# version 01
# baseline (speedup 1.0000x reference)
import numpy as np
import ml_dtypes
from contextlib import ExitStack

import concourse.bass as bass
import concourse.mybir as mybir
from concourse.bass_utils import run_bass_kernel_spmd

F32 = mybir.dt.float32
BF16 = mybir.dt.bfloat16
AF = mybir.ActivationFunctionType
OP = mybir.AluOpType
AX = mybir.AxisListType
bf16 = ml_dtypes.bfloat16

NCORE = 8
T = 4096
D = 1024
NT = T // 128

ENGS = ("tensor", "vector", "scalar", "gpsimd", "sync")


class Prog:
    def __init__(self, nc):
        self.nc = nc
        self.ops = []
        self.lastw = {}
        self.readers = {}
        self.psum_keys = set()

    def op(self, eng, fn, reads=(), writes=(), dma=False, tag=""):
        deps = set()
        for k in reads:
            i = self.lastw.get(k)
            if i is not None:
                deps.add(i)
            if k in self.psum_keys:
                for ridx, reng in self.readers.get(k, {}).items():
                    if reng != eng:
                        deps.add(ridx)
        for k in writes:
            i = self.lastw.get(k)
            if i is not None:
                deps.add(i)
            deps.update(self.readers.get(k, {}).keys())
        idx = len(self.ops)
        self.ops.append(dict(eng=eng, fn=fn, deps=deps, dma=dma, tag=tag,
                             rd=frozenset(reads), wr=frozenset(writes)))
        for k in reads:
            self.readers.setdefault(k, {})[idx] = eng
        for k in writes:
            self.lastw[k] = idx
            self.readers[k] = {}
        return idx

    def dbg(self, name, src_ap, reads, shape, dt):
        if not getattr(self, "debug", False):
            return
        d = self.nc.dram_tensor("dbg_" + name, list(shape), dt,
                                kind="ExternalOutput").ap()
        self.dma("sync", d, src_ap, reads=reads, tag="dbg_" + name)

    def dma(self, queue, dst, src, reads=(), writes=(), tag=""):
        fn = lambda e: e.dma_start(dst, src)
        return self.op(queue, fn, reads=reads, writes=writes, dma=True, tag=tag)

    def finalize_and_emit(self, st):
        nc = self.nc
        n = len(self.ops)
        eng_of = [o["eng"] for o in self.ops]
        edges = []
        for i, o in enumerate(self.ops):
            for d in o["deps"]:
                if eng_of[d] != o["eng"] or self.ops[d]["dma"]:
                    edges.append((d, i))
        needs_inc = set()
        cons_of = {}
        for d, i in edges:
            cons_of.setdefault(i, []).append(d)
            if not self.ops[d]["dma"]:
                needs_inc.add(d)
        NLANE = 6
        inc_idx = {}
        eng_count = {e: 0 for e in ENGS}
        dma_sem = {}
        dma_idx = {}
        lane_count = {}
        dma_seq = {e: 0 for e in ENGS}
        for i, o in enumerate(self.ops):
            if o["dma"]:
                e = o["eng"]
                lane = dma_seq[e] % NLANE
                dma_seq[e] += 1
                key = (e, lane)
                lane_count[key] = lane_count.get(key, 0) + 1
                dma_sem[i] = f"dma_{e}_{lane}"
                dma_idx[i] = 16 * lane_count[key]
            elif i in needs_inc:
                eng_count[o["eng"]] += 1
                inc_idx[i] = eng_count[o["eng"]]
        sems = {}
        for e in ENGS:
            if eng_count[e] > 0:
                sems[e] = st.enter_context(nc.semaphore(f"sem_{e}"))
        for (e, lane), cnt in lane_count.items():
            sems[f"dma_{e}_{lane}"] = st.enter_context(
                nc.semaphore(f"dsem_{e}_{lane}"))
        waited = {e: {} for e in ENGS}
        waits_for = {}
        for i, o in enumerate(self.ops):
            prods = cons_of.get(i, ())
            req = {}
            for d in prods:
                if self.ops[d]["dma"]:
                    sname, val = dma_sem[d], dma_idx[d]
                else:
                    sname, val = eng_of[d], inc_idx[d]
                if req.get(sname, -1) < val:
                    req[sname] = val
            if o["dma"] and dma_idx[i] > 16:
                sname = dma_sem[i]
                if req.get(sname, -1) < dma_idx[i] - 16:
                    req[sname] = dma_idx[i] - 16
            wl = []
            wd = waited[o["eng"]]
            for sname, val in req.items():
                if wd.get(sname, -1) < val:
                    wd[sname] = val
                    wl.append((sname, val))
            if wl:
                waits_for[i] = wl
        self._sems = sems
        self._trace = (inc_idx, dma_idx, waits_for)

        for sname in sorted(sems):
            nc.sync.sem_clear(sems[sname])
        nc.all_engine_barrier()
        with nc.sbuf_tensor("act_warm", [128, 32], F32) as warm:
            for fn_ in getattr(self, "act_warm",
                               (AF.Exp, AF.Ln, AF.Square, AF.Silu, AF.Tanh)):
                nc.scalar.activation(warm[:], warm[:], fn_)
        nwarm = getattr(self, "warmup_dmas", 101)
        if nwarm and getattr(self, "warm_src", None) is not None:
            with (nc.sbuf_tensor("dma_warm", [1, 32], F32) as wtile,
                  nc.semaphore("warm_sem") as wsem):
                nc.sync.sem_clear(wsem)
                for wi in range(nwarm):
                    nc.sync.dma_start(wtile[0:1, :],
                                      self.warm_src).then_inc(wsem, 16)
                    nc.sync.wait_ge(wsem, 16 * (wi + 1))
        nc.all_engine_barrier()

        engs = {e: getattr(nc, e) for e in ENGS}
        dummy_a = st.enter_context(nc.sbuf_tensor("hz_a", [128, 2], F32))
        dummy_b = st.enter_context(nc.sbuf_tensor("hz_b", [128, 2], F32))
        prev_w = {e: frozenset() for e in ENGS}
        for i, o in enumerate(self.ops):
            engobj = engs[o["eng"]]
            if (o["eng"] in ("vector", "gpsimd") and not o["dma"]
                    and (o["rd"] & prev_w[o["eng"]])):
                engobj.tensor_copy(dummy_a[:, 0:1], dummy_b[:, 0:1])
            prev_w[o["eng"]] = o["wr"] if not o["dma"] else frozenset()
            for sname, val in waits_for.get(i, ()):
                engobj.wait_ge(sems[sname], val)
            ins = o["fn"](engobj)
            if o["dma"]:
                ins.then_inc(sems[dma_sem[i]], 16)
            elif i in inc_idx:
                ins.then_inc(sems[o["eng"]], 1)
        for (e, lane), cnt in lane_count.items():
            engs[e].wait_ge(sems[f"dma_{e}_{lane}"], 16 * cnt)


class Pool:
    def __init__(self, nc, st):
        self.nc, self.st = nc, st
        self.sbufs = {}
        self.psums = {}

    def sb(self, role, cols, dt=F32, parts=128):
        key = (role, dt)
        if key not in self.sbufs:
            self.sbufs[key] = self.st.enter_context(
                self.nc.sbuf_tensor(f"sb_{role}", [parts, cols], dt))
        t = self.sbufs[key]
        assert t.shape[1] >= cols, (role, t.shape, cols)
        return t

    def ps(self, role, cols=512, dt=F32):
        key = role
        if key not in self.psums:
            self.psums[key] = self.st.enter_context(
                self.nc.psum_tensor(f"ps_{role}", [128, cols], dt))
        return self.psums[key]


def _host_consts():
    c = {}
    c["IDENT"] = np.eye(128, dtype=bf16)
    kr = np.arange(128)[:, None]
    col = np.arange(1152)[None, :]
    c["L0M"] = (np.abs(col - 512 - kr) <= 128).astype(bf16)
    col2 = np.arange(1024)[None, :]
    c["L1M"] = ((col2 - 512) >= kr).astype(bf16)
    c["ONESBF"] = np.ones((1, 128), dtype=bf16)
    c["ONESF"] = np.ones((1, 128), dtype=np.float32)
    c["ONESFC"] = np.ones((128, 1), dtype=np.float32)
    c["ONESBC"] = np.ones((128, 1), dtype=bf16)
    c["ONESALL"] = np.ones((128, 128), dtype=bf16)
    kk_ = np.arange(128)[:, None] // 64
    hh_ = np.arange(16)[None, :] % 2
    c["HSEL"] = (kk_ == hh_).astype(np.float32)
    c["IOTA9"] = ((np.arange(8, dtype=np.float32)[None, :] + 16.0)
                  * np.ones((128, 1), np.float32))
    c["IOTA"] = (np.arange(8, dtype=np.float32)[None, :]
                 * np.ones((128, 1), np.float32))
    return c


def _declare_consts(nc, pool, p, needed=None):
    aps = {}
    specs = {"IDENT": ([128, 128], BF16), "L0M": ([128, 1152], BF16),
             "L1M": ([128, 1024], BF16), "ONESBF": ([1, 128], BF16),
             "ONESF": ([1, 128], F32), "ONESFC": ([128, 1], F32),
             "ONESBC": ([128, 1], BF16), "ONESALL": ([128, 128], BF16),
             "HSEL": ([128, 16], F32),
             "IOTA9": ([128, 8], F32),
             "IOTA": ([128, 8], F32)}
    if needed is not None:
        specs = {k: v for k, v in specs.items() if k in needed}
    sb = {}
    for name, (shape, dt) in specs.items():
        aps[name] = nc.dram_tensor(name, shape, dt, kind="ExternalInput").ap()
        sb[name] = pool.sb("c_" + name, shape[1], dt, parts=shape[0])
        p.dma("sync", sb[name][:], aps[name], writes=[("c", name)], tag="const")
    return sb


def emit_ln_epilogue(p, pool, y_key, y, out_ap, out_key, dma_queue="sync",
                     sq_t=None, sq_k=None, yln_t=None, yln_k=None):
    nc = p.nc
    smA = pool.sb("lnA", 1, F32)
    smB = pool.sb("lnB", 1, F32)
    smC = pool.sb("lnC", 1, F32)
    smD = pool.sb("lnD", 1, F32)
    smE = pool.sb("lnE", 1, F32)
    smG = pool.sb("lnG", 1, F32)
    smF = pool.sb("lnF", 1, F32)
    eps = pool.sb("ln_eps", 1, F32)
    smk = ("ln_small",)
    if not getattr(pool, "_eps_done", False):
        pool._eps_done = True
        p.op("vector", lambda e: e.memset(eps[:], 1.0e-5), writes=[("ln_eps",)])
    sq = pool.sb("ln_sq", 1024, F32) if sq_t is None else sq_t
    sqk = ("ln_sq",) if sq_k is None else sq_k
    yln = pool.sb("ln_out", 1024, F32) if yln_t is None else yln_t
    ylnk = ("ln_out",) if yln_k is None else yln_k
    y_ap = y if isinstance(y, bass.AP) else y[:]
    p.op("vector", lambda e: e.reduce_sum(smA[:], y_ap, AX.X),
         reads=[y_key], writes=[smk], tag="ln_sum")
    p.op("scalar", lambda e: e.activation(sq[:], y_ap, AF.Square),
         reads=[y_key], writes=[sqk], tag="ln_sq")
    p.op("vector", lambda e: e.reduce_sum(smC[:], sq[:], AX.X),
         reads=[sqk], writes=[smk], tag="ln_ssq")
    p.op("vector", lambda e: e.tensor_scalar(smB[:], smA[:], 1.0 / D,
                                             None, OP.mult),
         reads=[smk], writes=[smk], tag="ln_mean")
    p.op("scalar", lambda e: e.activation(smD[:], smB[:], AF.Square),
         reads=[smk], writes=[smk], tag="ln_msq")
    p.op("vector", lambda e: e.scalar_tensor_tensor(smE[:], smC[:],
                                                    1.0 / D, smD[:],
                                                    OP.mult, OP.subtract),
         reads=[smk], writes=[smk], tag="ln_var")
    p.op("scalar", lambda e: e.activation(smF[:], smE[:], AF.Ln,
                                          bias=eps[:, 0:1]),
         reads=[smk, ("ln_eps",)], writes=[smk], tag="ln_lv")
    p.op("scalar", lambda e: e.activation(smG[:], smF[:], AF.Exp,
                                          scale=-0.5),
         reads=[smk], writes=[smk], tag="ln_rstd")
    p.op("vector", lambda e: e.tensor_scalar(yln[:], y_ap, smB[:, 0:1],
                                             smG[:, 0:1], OP.subtract, OP.mult),
         reads=[y_key, smk], writes=[ylnk], tag="ln_norm")
    if getattr(p, "debug", False) and not getattr(p, "_ln_dbg_done", False):
        p._ln_dbg_done = True
        p.dbg("ln_yy", y[:], [y_key], [128, 1024], F32)
        p.dbg("ln_sq", sq[:], [sqk], [128, 1024], F32)
        p.dbg("ln_smA", smA[:], [smk], [128, 1], F32)
        p.dbg("ln_smB", smB[:], [smk], [128, 1], F32)
        p.dbg("ln_smC", smC[:], [smk], [128, 1], F32)
        p.dbg("ln_smD", smD[:], [smk], [128, 1], F32)
        p.dbg("ln_smE", smE[:], [smk], [128, 1], F32)
        p.dbg("ln_smG", smG[:], [smk], [128, 1], F32)
        p.dbg("ln_yln", yln[:], [ylnk], [128, 1024], F32)
    p.dma(dma_queue, out_ap, yln[:], reads=[ylnk], writes=[out_key],
          tag="ln_out_dma")


def load_weight_bf16(p, pool, dst, dst_key, dram_ap, rows, cols, role_stage,
                     queue="gpsimd", stage_cols=4096):
    nch = rows // 128
    stg = pool.sb(role_stage, stage_cols, F32)
    stg_k = (role_stage,)
    per = max(1, stage_cols // cols)
    c = 0
    while c < nch:
        cn = min(per, nch - c)
        src = dram_ap.rearrange("(c p) n -> p c n", p=128)[:, c:c + cn, :]
        p.dma(queue, stg[:, 0:cn * cols].rearrange("p (c n) -> p c n", n=cols),
              src, writes=[stg_k], tag="wload")
        p.op("vector",
             lambda e, c=c, cn=cn: e.tensor_copy(
                 dst[:, cols * c:cols * (c + cn)], stg[:, 0:cols * cn]),
             reads=[stg_k], writes=[dst_key], tag="wcast")
        c += cn


def emit_l0_attn(p, pool, C, x_dram, out_dram, w_qkv_d, w_proj_d, b_proj_d, groups=range(8)):
    nc = p.nc
    H, Dh = 16, 64
    wqkv = pool.sb("wbig", 8 * 3072, BF16)
    load_weight_bf16(p, pool, wqkv, ("wbig",), w_qkv_d, 1024, 3072, "stage")
    wproj = pool.sb("bf8kA", 8192, BF16)
    load_weight_bf16(p, pool, wproj, ("bf8kA",), w_proj_d, 1024, 1024, "stage")
    bproj = pool.sb("biasrow", 1024, BF16, parts=1)
    bstg = pool.sb("stage", 4096, F32)
    p.dma("gpsimd", bstg[0:1, 0:1024], b_proj_d.rearrange("(a n) -> a n", a=1),
          writes=[("stage",)])
    p.op("vector", lambda e: e.tensor_copy(bproj[0:1, :], bstg[0:1, 0:1024]),
         reads=[("stage",)], writes=[("biasrow",)])

    biasc = pool.sb("biasc", 1, F32)
    epsl = pool.sb("epsl", 1, F32)
    if not getattr(pool, "_attc_done", False):
        pool._attc_done = True
        p.op("vector", lambda e: e.memset(biasc[:], -25.0), writes=[("biasc",)])
        p.op("vector", lambda e: e.memset(epsl[:], 1.0e-26), writes=[("epsl",)])
    xfm = pool.sb("xfm6", 6 * 1024, BF16)
    kfm = pool.sb("kfm6", 6 * 1024, BF16)
    qfm = pool.sb("qfm", 4096, BF16)
    vau = [pool.sb(f"vau{s}", 2048, BF16) for s in range(6)]
    ctx = pool.sb("ctxfm", 4096, BF16)
    lnl = pool.sb("lnl", 512, F32, parts=64)
    rbf = pool.sb("rbf", 512, F32, parts=64)
    cxf = pool.sb("cxf", 512, F32, parts=64)
    pbf = [pool.sb(f"pbf{i}", 512, BF16) for i in range(2)]
    att = pool.sb("att", 1024, F32)
    yy = pool.sb("yy", 1024, F32)

    pS = [pool.ps("pA"), pool.ps("pB")]
    pC_ = pool.ps("pC")
    pM = [pool.ps("pD"), pool.ps("pE")]
    pX = pool.ps("pF")
    ones_bf = C["ONESBF"]
    ones_f = C["ONESF"]

    for g in groups:
        kt0 = 4 * g - 1
        slots = [s for s in range(6) if 0 <= kt0 + s < 32]
        xin = {}
        tiles = []
        for s in slots:
            kt = kt0 + s
            xt = pool.sb(f"xin{s}", 1024, F32)
            xk = ("xin", s)
            p.dma("gpsimd", xt[:], x_dram[128 * kt:128 * (kt + 1), :],
                  writes=[xk], tag=f"xload{g}")
            xin[s] = (xt, xk)
            tiles.append((s, (xt[:], xk)))
        xfm_k = ("xfm",)
        pT = pool.ps("pT", 1024, BF16)
        for s_missing in [s for s in range(6) if s not in slots]:
            p.op("vector",
                 lambda e, s=s_missing: e.memset(
                     xfm[:].rearrange("p (c q) -> p c q", q=768)
                     [:, :, 128 * s:128 * (s + 1)], 0.0),
                 writes=[xfm_k], tag="xzero")
        for s, (xt_ap, xk) in tiles:
            xb = pool.sb(f"xbf{s % 2}", 1024, BF16)
            xbk = ("xbf", s % 2)
            p.op("vector", lambda e, o=xb, i=xt_ap: e.tensor_copy(o[:], i),
                 reads=[xk], writes=[xbk], tag="xcast")
            for c in range(8):
                p.op("tensor",
                     lambda e, o=pT, i=xb, c=c: e.transpose(
                         o[:, 128 * c:128 * (c + 1)],
                         i[:, 128 * c:128 * (c + 1)], C["IDENT"][:]),
                     reads=[xbk, ("c", "IDENT")], writes=[("ps", "pT")])
            p.op("scalar",
                 lambda e, s=s: e.copy(
                     xfm[:].rearrange("p (c q) -> p c q", c=6 + 2)[:, :, 0:0]
                     if False else
                     xfm[:].rearrange("p (c q) -> p c q", q=768)
                     [:, :, 128 * s:128 * (s + 1)],
                     pT[:].rearrange("p (c q) -> p c q", q=128)),
                 reads=[("ps", "pT")], writes=[xfm_k], tag="xevac")

        if g == 0:
            p.dbg("xfm", xfm[:], [xfm_k], [128, 6144], BF16)
        smin, smax = slots[0], slots[-1]
        for c in range(8):
            for half, (n0, nn) in enumerate([(0, 512), (512, 256)]):
                if 128 * smin >= n0 + nn or 128 * (smax + 1) <= n0:
                    pass
                pp = pM[half]
                ppk = ("ps", "pD" if half == 0 else "pE")
                for ic in range(8):
                    p.op("tensor",
                         lambda e, pp=pp, c=c, ic=ic, n0=n0, nn=nn:
                         e.matmul(pp[:, 0:nn],
                                  wqkv[:, 3072 * ic + 1024 + 128 * c:
                                       3072 * ic + 1024 + 128 * (c + 1)],
                                  xfm[:, 768 * ic + n0:768 * ic + n0 + nn],
                                  start=(ic == 0), stop=(ic == 7)),
                         reads=[("wbig",), xfm_k], writes=[ppk], tag="Kmm")
                p.op("scalar",
                     lambda e, c=c, n0=n0, nn=nn, pp=pp:
                     e.copy(kfm[:, 768 * c + n0:768 * c + n0 + nn], pp[:, 0:nn]),
                     reads=[ppk], writes=[("kfm",)], tag="Kevac")
        for s in slots:
            for half in range(2):
                pp = pM[half]
                ppk = ("ps", "pD" if half == 0 else "pE")
                for ic in range(8):
                    p.op("tensor",
                         lambda e, pp=pp, s=s, ic=ic, half=half:
                         e.matmul(pp[:],
                                  xfm[:, 768 * ic + 128 * s:768 * ic + 128 * (s + 1)],
                                  wqkv[:, 3072 * ic + 2048 + 512 * half:
                                       3072 * ic + 2048 + 512 * (half + 1)],
                                  start=(ic == 0), stop=(ic == 7)),
                         reads=[("wbig",), xfm_k], writes=[ppk], tag="Vmm")
                p.op("vector",
                     lambda e, s=s, half=half, pp=pp:
                     e.tensor_copy(
                         vau[s][:].rearrange("p (h d) -> p h d", d=128)
                         [:, 8 * half:8 * (half + 1), 0:64],
                         pp[:].rearrange("p (h d) -> p h d", d=64)),
                     reads=[ppk], writes=[("vau", s)], tag="Vevac")
            p.op("gpsimd",
                 lambda e, s=s: e.memset(
                     vau[s][:].rearrange("p (h d) -> p h d", d=128)[:, :, 64:128], 1.0),
                 writes=[("vau", s)], tag="Vones")
        for c in range(8):
            pp = pM[0]
            ppk = ("ps", "pD")
            for ic in range(8):
                p.op("tensor",
                     lambda e, c=c, ic=ic, pp=pp:
                     e.matmul(pp[:],
                              wqkv[:, 3072 * ic + 128 * c:3072 * ic + 128 * (c + 1)],
                              xfm[:, 768 * ic + 128:768 * ic + 640],
                              start=(ic == 0), stop=(ic == 7)),
                     reads=[("wbig",), xfm_k], writes=[ppk], tag="Qmm")
            p.op("scalar",
                 lambda e, c=c, pp=pp: e.copy(qfm[:, 512 * c:512 * (c + 1)], pp[:]),
                 reads=[ppk], writes=[("qfm",)], tag="Qevac")

        if g == 0:
            p.dbg("kfm", kfm[:], [("kfm",)], [128, 6144], BF16)
            p.dbg("qfm", qfm[:], [("qfm",)], [128, 4096], BF16)
            p.dbg("vau1", vau[1][:], [("vau", 1)], [128, 2048], BF16)
        for h in range(H):
            c, base = h // 2, 64 * (h % 2)
            for j, s in enumerate(slots):
                pp = pS[j % 2]
                ppk = ("ps", "pA" if j % 2 == 0 else "pB")
                p.op("tensor",
                     lambda e, pp=pp, c=c, base=base, s=s:
                     e.matmul(pp[:],
                              kfm[base:base + 64,
                                  768 * c + 128 * s:768 * c + 128 * (s + 1)],
                              qfm[base:base + 64, 512 * c:512 * (c + 1)],
                              start=True, stop=True),
                     reads=[("kfm",), ("qfm",)], writes=[ppk], tag="Smm")
                pb = pbf[j % 2]
                pbk = ("pbf", j % 2)
                p.op("scalar",
                     lambda e, pp=pp, pb=pb: e.activation(
                         pb[:], pp[:], AF.Exp, bias=biasc[:, 0:1], scale=0.125),
                     reads=[ppk, ("biasc",)], writes=[pbk], tag="exp")
                p.op("vector",
                     lambda e, pb=pb, s=s: e.tensor_mul(
                         pb[:], pb[:], C["L0M"][:, 640 - 128 * s:1152 - 128 * s]),
                     reads=[pbk, ("c", "L0M")], writes=[pbk], tag="maskmul")
                p.op("tensor",
                     lambda e, pb=pb, s=s, h=h, j=j:
                     e.matmul(pC_[:], vau[s][:, 128 * h:128 * (h + 1)],
                              pb[:], start=(j == 0), stop=(j == len(slots) - 1)),
                     reads=[pbk, ("vau", s)], writes=[("ps", "pC")], tag="PV")
            p.op("scalar",
                 lambda e: e.activation(lnl[0:64, :], pC_[64:128, :], AF.Ln,
                                        bias=epsl[0:64, 0:1]),
                 reads=[("ps", "pC"), ("epsl",)], writes=[("lnl",)], tag="lnl")
            p.op("scalar",
                 lambda e: e.activation(rbf[0:64, :], lnl[0:64, :], AF.Exp,
                                        scale=-1.0),
                 reads=[("lnl",)], writes=[("rbf",)], tag="rexp")
            p.op("scalar",
                 lambda e: e.copy(cxf[0:64, :], pC_[0:64, :]),
                 reads=[("ps", "pC")], writes=[("cxf",)], tag="ctxevac")
            p.op("vector",
                 lambda e, c=c, base=base: e.tensor_mul(
                     ctx[base:base + 64, 512 * c:512 * (c + 1)],
                     cxf[0:64, :], rbf[0:64, :]),
                 reads=[("cxf",), ("rbf",)], writes=[("ctx",)], tag="ctxmul")
            if g == 0 and h == 0:
                p.dbg("pbf0", pbf[len(slots) % 2][:],
                      [("pbf", len(slots) % 2)], [128, 512], BF16)


        if g == 0:
            p.dbg("ctx", ctx[:], [("ctx",)], [128, 4096], BF16)
        for qt in range(4):
            s = qt + 1
            for half in range(2):
                pp = pM[half]
                ppk = ("ps", "pD" if half == 0 else "pE")
                for c in range(8):
                    p.op("tensor",
                         lambda e, pp=pp, c=c, qt=qt, half=half:
                         e.matmul(pp[:],
                                  ctx[:, 512 * c + 128 * qt:512 * c + 128 * (qt + 1)],
                                  wproj[:, 1024 * c + 512 * half:
                                        1024 * c + 512 * (half + 1)],
                                  start=(c == 0), stop=False),
                         reads=[("ctx",), ("bf8kA",)], writes=[ppk], tag="Omm")
                p.op("tensor",
                     lambda e, pp=pp, half=half: e.matmul(
                         pp[:], ones_bf[0:1, :],
                         bproj[0:1, 512 * half:512 * (half + 1)],
                         start=False, stop=True),
                     reads=[("biasrow",), ("c", "ONESBF")], writes=[ppk],
                     tag="Obias")
                p.op("scalar",
                     lambda e, pp=pp, half=half: e.copy(
                         att[:, 512 * half:512 * (half + 1)], pp[:]),
                     reads=[ppk], writes=[("att",)], tag="Oevac")
            if g == 0 and qt == 0:
                p.dbg("att", att[:], [("att",)], [128, 1024], F32)
            xt, xk = xin[s]
            p.op("vector",
                 lambda e, xt=xt: e.tensor_add(yy[:], xt[:], att[:]),
                 reads=[xk, ("att",)], writes=[("yy",)], tag="resid")
            kt = kt0 + s
            emit_ln_epilogue(p, pool, ("yy",), yy,
                             out_dram[128 * kt:128 * (kt + 1), :],
                             ("out", kt), sq_t=att, sq_k=("att",),
                             yln_t=xt, yln_k=xk)


def emit_cast_transpose_tile(p, pool, C, xt_ap, xin_key, xfm, xfm_key,
                             chunk_stride, tok_off, xbf_slot):
    nc = p.nc
    pT = pool.ps("pT", 1024, BF16)
    xb = pool.sb(f"xbf{xbf_slot}", 1024, BF16)
    xbk = ("xbf", xbf_slot)
    p.op("vector", lambda e: e.tensor_copy(xb[:], xt_ap),
         reads=[xin_key], writes=[xbk], tag="xcast")
    for c in range(8):
        p.op("tensor",
             lambda e, c=c: e.transpose(pT[:, 128 * c:128 * (c + 1)],
                                        xb[:, 128 * c:128 * (c + 1)],
                                        C["IDENT"][:]),
             reads=[xbk, ("c", "IDENT")], writes=[("ps", "pT")], tag="xT")
    p.op("scalar",
         lambda e: e.copy(
             xfm[:].rearrange("p (c q) -> p c q", q=chunk_stride)
             [:, :, tok_off:tok_off + 128],
             pT[:].rearrange("p (c q) -> p c q", q=128)),
         reads=[("ps", "pT")], writes=[xfm_key], tag="xevac")


def emit_moe(p, pool, C, x_dram, out_dram, wr_d, wg_d, wu_d, wd_d,
             sg_d, su_d, sd_d, ebias_d, tiles=range(4)):
    nc = p.nc
    E = 8
    wr = pool.sb("wr", 64, BF16)
    stg = pool.sb("stage", 2048, F32)
    p.dma("gpsimd", stg[:, 0:64].rearrange("p (c n) -> p c n", n=8),
          wr_d.rearrange("(c p) n -> p c n", p=128), writes=[("stage",)])
    p.op("vector", lambda e: e.tensor_copy(wr[:, 0:64], stg[:, 0:64]),
         reads=[("stage",)], writes=[("wr",)])
    ebias = pool.sb("ebias", 8, BF16, parts=1)
    p.dma("gpsimd", stg[0:1, 64:72], ebias_d.rearrange("(a n) -> a n", a=1),
          writes=[("stage",)], tag="ebload")
    p.op("vector", lambda e: e.tensor_copy(ebias[0:1, :], stg[0:1, 64:72]),
         reads=[("stage",)], writes=[("ebias",)])
    sgb = pool.sb("sgb", 8192, BF16)
    sub = pool.sb("sub", 8192, BF16)
    sdb = pool.sb("sdb", 8192, BF16)
    load_weight_bf16(p, pool, sgb, ("sgb",), sg_d, 1024, 1024, "stage",
                     stage_cols=2048)
    load_weight_bf16(p, pool, sub, ("sub",), su_d, 1024, 1024, "stage",
                     stage_cols=2048)
    load_weight_bf16(p, pool, sdb, ("sdb",), sd_d, 1024, 1024, "stage",
                     stage_cols=2048)

    xfm = pool.sb("xfm8", 8192, BF16)
    hfm = pool.sb("hfm", 4096, BF16)
    hsf = pool.sb("hsf", 8192, BF16)
    acc = pool.sb("acc", 8192, F32)
    wgb = [pool.sb(f"wgb{i}", 4096, BF16) for i in range(2)]
    wub = [pool.sb(f"wub{i}", 4096, BF16) for i in range(2)]
    wdb = [pool.sb(f"wdb{i}", 4096, BF16) for i in range(2)]
    stg2 = [pool.sb("stage", 2048, F32), pool.sb("stg2b", 2048, F32)]
    names = ["lg", "dd1", "aa1", "mm1", "ii1", "cc1", "dd2", "ab2", "mm2",
             "is1", "lm", "dd3", "aa3", "mm3", "ii3", "cc3", "dd4", "ab4",
             "mm4", "is2", "tt1", "tt2"]
    rt = {nm: pool.sb("rt_" + nm, 8, F32) for nm in names}
    rs = {nm: pool.sb("rs_" + nm, 1, F32) for nm in
          ["r1", "px1", "r2", "px2", "sd_", "th", "pp2", "pp1"]}
    rw = pool.sb("rw_all", 64, F32)
    sg_f = pool.sb("sg_f", 512, F32)
    up_f = pool.sb("up_f", 512, F32)
    tmp_f = pool.sb("tmp_f", 512, F32)
    xt2 = [pool.sb(f"xin{i}", 1024, F32) for i in range(2)]
    xep = xt2

    pM = [pool.ps("pD"), pool.ps("pE")]
    pR = pool.ps("pF")
    ones_bf = C["ONESBF"]

    def load_expert(e_, par):
        stgt = stg2[par]
        stk = ("stage",) if par == 0 else ("stg2", 1)
        for nm, dram, dstt, rows, cols in [
            ("wgb", wg_d, wgb[par], 1024, 512),
            ("wub", wu_d, wub[par], 1024, 512),
            ("wdb", wd_d, wdb[par], 512, 1024),
        ]:
            nch = rows // 128
            per = 2048 // cols
            c0 = 0
            while c0 < nch:
                cn = min(per, nch - c0)
                src = dram[e_].rearrange("(c p) n -> p c n", p=128)[:, c0:c0 + cn, :]
                p.dma("gpsimd",
                      stgt[:, 0:cn * cols].rearrange("p (c n) -> p c n", n=cols),
                      src, writes=[stk], tag="weload")
                p.op("vector",
                     lambda e, c0=c0, cn=cn, cols=cols, dstt=dstt:
                     e.tensor_copy(dstt[:, cols * c0:cols * (c0 + cn)],
                                   stgt[:, 0:cols * cn]),
                     reads=[stk], writes=[(nm, par)], tag="wecast")
                c0 += cn

    for ti, tt in enumerate(tiles):
        for sub_ in range(8):
            kt = 8 * tt + sub_
            xt = xt2[sub_ % 2]
            xk = ("xin", sub_ % 2)
            p.dma("gpsimd", xt[:], x_dram[128 * kt:128 * (kt + 1), :],
                  writes=[xk], tag="xload")
            emit_cast_transpose_tile(p, pool, C, xt[:], xk, xfm, ("xfm",),
                                     1024, 128 * sub_, sub_ % 2)
        p.op("gpsimd", lambda e: e.memset(acc[:], 0.0), writes=[("acc",)],
             tag="acczero")
        for sub_ in range(8):
            for c in range(8):
                p.op("tensor",
                     lambda e, c=c, sub_=sub_: e.matmul(
                         pR[:, 0:8],
                         xfm[:, 1024 * c + 128 * sub_:1024 * c + 128 * (sub_ + 1)],
                         wr[:, 8 * c:8 * (c + 1)],
                         start=(c == 0), stop=False),
                     reads=[("xfm",), ("wr",)], writes=[("ps", "pF")], tag="Rmm")
            p.op("tensor",
                 lambda e: e.matmul(pR[:, 0:8], ones_bf[0:1, :], ebias[0:1, :],
                                    start=False, stop=True),
                 reads=[("ebias",), ("c", "ONESBF")], writes=[("ps", "pF")],
                 tag="Rbias")
            p.op("vector", lambda e: e.tensor_copy(rt["lg"][:, 0:8], pR[:, 0:8]),
                 reads=[("ps", "pF")], writes=[("rt", "lg")], tag="Revac")
            t = rt
            def vop(fn, outs, ins, tag):
                p.op("vector", fn, reads=[("rt", n) for n in ins],
                     writes=[("rt", n) for n in outs], tag=tag)
            vop(lambda e: e.reduce_max(rs["r1"][:], t["lg"][:, 0:8], AX.X),
                ["r1"], ["lg"], "t1max")
            vop(lambda e: e.tensor_scalar(t["dd1"][:, 0:8], t["lg"][:, 0:8],
                                          rs["r1"][:, 0:1], None, OP.subtract),
                ["dd1"], ["lg", "r1"], "t1d")
            p.op("scalar", lambda e: e.activation(t["aa1"][:, 0:8],
                                                  t["dd1"][:, 0:8], AF.Abs,
                                                  scale=1.0e38),
                 reads=[("rt", "dd1")], writes=[("rt", "aa1")], tag="t1abs")
            vop(lambda e: e.tensor_scalar(t["mm1"][:, 0:8], t["aa1"][:, 0:8],
                                          1.0, -1.0, OP.min, OP.mult),
                ["mm1"], ["aa1"], "t1min")
            vop(lambda e: e.tensor_scalar(t["ii1"][:, 0:8], t["mm1"][:, 0:8],
                                          1.0, None, OP.add),
                ["ii1"], ["mm1"], "t1ind")
            vop(lambda e: e.scalar_tensor_tensor(t["cc1"][:, 0:8],
                                                 t["ii1"][:, 0:8], -16.0,
                                                 C["IOTA9"][:, 0:8], OP.mult,
                                                 OP.add),
                ["cc1"], ["ii1"], "t1cand")
            vop(lambda e: e.tensor_reduce(rs["px1"][:], t["cc1"][:, 0:8], AX.X,
                                          OP.min),
                ["px1"], ["cc1"], "t1idx")
            vop(lambda e: e.tensor_scalar(t["dd2"][:, 0:8], C["IOTA"][:, 0:8],
                                          rs["px1"][:, 0:1], None, OP.subtract),
                ["dd2"], ["px1"], "t1d2")
            p.op("scalar", lambda e: e.activation(t["ab2"][:, 0:8],
                                                  t["dd2"][:, 0:8], AF.Abs),
                 reads=[("rt", "dd2")], writes=[("rt", "ab2")], tag="t1abs2")
            vop(lambda e: e.tensor_scalar(t["mm2"][:, 0:8], t["ab2"][:, 0:8],
                                          1.0, -1.0, OP.min, OP.mult),
                ["mm2"], ["ab2"], "t1min2")
            vop(lambda e: e.tensor_scalar(t["is1"][:, 0:8], t["mm2"][:, 0:8],
                                          1.0, None, OP.add),
                ["is1"], ["mm2"], "t1is")
            vop(lambda e: e.scalar_tensor_tensor(t["lm"][:, 0:8],
                                                 t["is1"][:, 0:8], -1.0e30,
                                                 t["lg"][:, 0:8], OP.mult,
                                                 OP.add),
                ["lm"], ["is1", "lg"], "t2mask")
            vop(lambda e: e.reduce_max(rs["r2"][:], t["lm"][:, 0:8], AX.X),
                ["r2"], ["lm"], "t2max")
            vop(lambda e: e.tensor_scalar(t["dd3"][:, 0:8], t["lm"][:, 0:8],
                                          rs["r2"][:, 0:1], None, OP.subtract),
                ["dd3"], ["lm", "r2"], "t2d")
            p.op("scalar", lambda e: e.activation(t["aa3"][:, 0:8],
                                                  t["dd3"][:, 0:8], AF.Abs,
                                                  scale=1.0e38),
                 reads=[("rt", "dd3")], writes=[("rt", "aa3")], tag="t2abs")
            vop(lambda e: e.tensor_scalar(t["mm3"][:, 0:8], t["aa3"][:, 0:8],
                                          1.0, -1.0, OP.min, OP.mult),
                ["mm3"], ["aa3"], "t2min")
            vop(lambda e: e.tensor_scalar(t["ii3"][:, 0:8], t["mm3"][:, 0:8],
                                          1.0, None, OP.add),
                ["ii3"], ["mm3"], "t2ind")
            vop(lambda e: e.scalar_tensor_tensor(t["cc3"][:, 0:8],
                                                 t["ii3"][:, 0:8], -16.0,
                                                 C["IOTA9"][:, 0:8], OP.mult,
                                                 OP.add),
                ["cc3"], ["ii3"], "t2cand")
            vop(lambda e: e.tensor_reduce(rs["px2"][:], t["cc3"][:, 0:8], AX.X,
                                          OP.min),
                ["px2"], ["cc3"], "t2idx")
            vop(lambda e: e.tensor_scalar(t["dd4"][:, 0:8], C["IOTA"][:, 0:8],
                                          rs["px2"][:, 0:1], None, OP.subtract),
                ["dd4"], ["px2"], "t2d2")
            p.op("scalar", lambda e: e.activation(t["ab4"][:, 0:8],
                                                  t["dd4"][:, 0:8], AF.Abs),
                 reads=[("rt", "dd4")], writes=[("rt", "ab4")], tag="t2abs2")
            vop(lambda e: e.tensor_scalar(t["mm4"][:, 0:8], t["ab4"][:, 0:8],
                                          1.0, -1.0, OP.min, OP.mult),
                ["mm4"], ["ab4"], "t2min2")
            vop(lambda e: e.tensor_scalar(t["is2"][:, 0:8], t["mm4"][:, 0:8],
                                          1.0, None, OP.add),
                ["is2"], ["mm4"], "t2is")
            vop(lambda e: e.tensor_sub(rs["sd_"][:], rs["r2"][:], rs["r1"][:]),
                ["sd_"], ["r2", "r1"], "pdif")
            p.op("scalar", lambda e: e.activation(rs["th"][:], rs["sd_"][:],
                                                  AF.Tanh, scale=0.5),
                 reads=[("rt", "sd_")], writes=[("rt", "th")], tag="ptanh")
            vop(lambda e: e.tensor_scalar(rs["pp2"][:], rs["th"][:], 0.5, 0.5,
                                          OP.mult, OP.add),
                ["pp2"], ["th"], "pp2")
            vop(lambda e: e.tensor_scalar(rs["pp1"][:], rs["pp2"][:], -1.0, 1.0,
                                          OP.mult, OP.add),
                ["pp1"], ["pp2"], "pp1")
            vop(lambda e: e.tensor_scalar(t["tt1"][:, 0:8], t["is1"][:, 0:8],
                                          rs["pp1"][:, 0:1], None, OP.mult),
                ["tt1"], ["is1", "pp1"], "rw1")
            vop(lambda e: e.tensor_scalar(t["tt2"][:, 0:8], t["is2"][:, 0:8],
                                          rs["pp2"][:, 0:1], None, OP.mult),
                ["tt2"], ["is2", "pp2"], "rw2")
            p.op("vector",
                 lambda e, sub_=sub_: e.tensor_add(
                     rw[:, 8 * sub_:8 * (sub_ + 1)], t["tt1"][:, 0:8],
                     t["tt2"][:, 0:8]),
                 reads=[("rt", "tt1"), ("rt", "tt2")], writes=[("rw",)],
                 tag="rwadd")
        if ti == 0:
            p.dbg("xfm2", xfm[:], [("xfm",)], [128, 8192], BF16)
            p.dbg("rw", rw[:], [("rw",)], [128, 64], F32)
            p.dbg("lg0", rt["lg"][:], [("rt", "lg")], [128, 8], F32)
            for nm in ("aa1", "mm1", "ii1", "cc1", "is1", "lm", "is2"):
                p.dbg("rt_" + nm, rt[nm][:, 0:8], [("rt", nm)], [128, 8], F32)
            for nm in ("r1", "px1", "r2", "px2", "pp1", "pp2"):
                p.dbg("rs_" + nm, rs[nm][:, 0:1], [("rt", nm)], [128, 1], F32)
        load_expert(0, 0)
        for e_ in range(E):
            par = e_ % 2
            if e_ + 1 < E:
                load_expert(e_ + 1, 1 - par)
            wgt, wut, wdt = wgb[par], wub[par], wdb[par]
            wk_ = [("wgb", par), ("wub", par), ("wdb", par)]
            for fc in range(4):
                for half in range(2):
                    for which, wt, wkk, pp, ppk in [
                        (0, wgt, wk_[0], pM[0], ("ps", "pD")),
                        (1, wut, wk_[1], pM[1], ("ps", "pE")),
                    ]:
                        for ic in range(8):
                            p.op("tensor",
                                 lambda e, wt=wt, fc=fc, half=half, ic=ic, pp=pp:
                                 e.matmul(pp[:],
                                          wt[:, 512 * ic + 128 * fc:
                                             512 * ic + 128 * fc + 128],
                                          xfm[:, 1024 * ic + 512 * half:
                                              1024 * ic + 512 * (half + 1)],
                                          start=(ic == 0), stop=(ic == 7)),
                                 reads=[wkk, ("xfm",)], writes=[ppk], tag="GUmm")
                    p.op("scalar",
                         lambda e: e.activation(sg_f[:], pM[0][:], AF.Silu),
                         reads=[("ps", "pD")], writes=[("sg_f",)], tag="silu")
                    p.op("vector",
                         lambda e: e.tensor_copy(up_f[:], pM[1][:]),
                         reads=[("ps", "pE")], writes=[("up_f",)], tag="upevac")
                    p.op("vector",
                         lambda e, fc=fc, half=half: e.tensor_mul(
                             hfm[:, 1024 * fc + 512 * half:
                                 1024 * fc + 512 * (half + 1)],
                             sg_f[:], up_f[:]),
                         reads=[("sg_f",), ("up_f",)], writes=[("hfm",)],
                         tag="hmul")
            for sub_ in range(8):
                for half in range(2):
                    pp = pM[half]
                    ppk = ("ps", "pD" if half == 0 else "pE")
                    for fc in range(4):
                        p.op("tensor",
                             lambda e, fc=fc, sub_=sub_, half=half, pp=pp,
                             wdt=wdt:
                             e.matmul(pp[:],
                                      hfm[:, 1024 * fc + 128 * sub_:
                                          1024 * fc + 128 * (sub_ + 1)],
                                      wdt[:, 1024 * fc + 512 * half:
                                          1024 * fc + 512 * (half + 1)],
                                      start=(fc == 0), stop=(fc == 3)),
                             reads=[wk_[2], ("hfm",)], writes=[ppk], tag="Dmm")
                    p.op("scalar",
                         lambda e, pp=pp, sub_=sub_, e_=e_: e.activation(
                             tmp_f[:], pp[:], AF.Copy,
                             scale=rw[:, 8 * sub_ + e_:8 * sub_ + e_ + 1]),
                         reads=[ppk, ("rw",)], writes=[("tmp_f",)], tag="dscale")
                    p.op("vector",
                         lambda e, sub_=sub_, half=half: e.tensor_add(
                             acc[:, 1024 * sub_ + 512 * half:
                                 1024 * sub_ + 512 * (half + 1)],
                             acc[:, 1024 * sub_ + 512 * half:
                                 1024 * sub_ + 512 * (half + 1)],
                             tmp_f[:]),
                         reads=[("acc",), ("tmp_f",)], writes=[("acc",)],
                         tag="dacc")
        if ti == 0:
            p.dbg("hfm7", hfm[:], [("hfm",)], [128, 4096], BF16)
            p.dbg("accE", acc[:], [("acc",)], [128, 8192], F32)
        for fc in range(8):
            for half in range(2):
                for which, wt, wkk, pp, ppk in [
                    (0, sgb, ("sgb",), pM[0], ("ps", "pD")),
                    (1, sub, ("sub",), pM[1], ("ps", "pE")),
                ]:
                    for ic in range(8):
                        p.op("tensor",
                             lambda e, wt=wt, fc=fc, half=half, ic=ic, pp=pp:
                             e.matmul(pp[:],
                                      wt[:, 1024 * ic + 128 * fc:
                                         1024 * ic + 128 * fc + 128],
                                      xfm[:, 1024 * ic + 512 * half:
                                          1024 * ic + 512 * (half + 1)],
                                      start=(ic == 0), stop=(ic == 7)),
                             reads=[wkk, ("xfm",)], writes=[ppk], tag="SGUmm")
                p.op("scalar",
                     lambda e: e.activation(sg_f[:], pM[0][:], AF.Silu),
                     reads=[("ps", "pD")], writes=[("sg_f",)], tag="ssilu")
                p.op("vector",
                     lambda e: e.tensor_copy(up_f[:], pM[1][:]),
                     reads=[("ps", "pE")], writes=[("up_f",)], tag="supevac")
                p.op("vector",
                     lambda e, fc=fc, half=half: e.tensor_mul(
                         hsf[:, 1024 * fc + 512 * half:
                             1024 * fc + 512 * (half + 1)],
                         sg_f[:], up_f[:]),
                     reads=[("sg_f",), ("up_f",)], writes=[("hsf",)], tag="shmul")
        for sub_ in range(8):
            for half in range(2):
                pp = pM[half]
                ppk = ("ps", "pD" if half == 0 else "pE")
                for fc in range(8):
                    p.op("tensor",
                         lambda e, fc=fc, sub_=sub_, half=half, pp=pp:
                         e.matmul(pp[:],
                                  hsf[:, 1024 * fc + 128 * sub_:
                                      1024 * fc + 128 * (sub_ + 1)],
                                  sdb[:, 1024 * fc + 512 * half:
                                      1024 * fc + 512 * (half + 1)],
                                  start=(fc == 0), stop=(fc == 7)),
                         reads=[("sdb",), ("hsf",)], writes=[ppk], tag="SDmm")
                p.op("scalar",
                     lambda e, pp=pp: e.copy(tmp_f[:], pp[:]),
                     reads=[ppk], writes=[("tmp_f",)], tag="sdevac")
                p.op("vector",
                     lambda e, sub_=sub_, half=half: e.tensor_add(
                         acc[:, 1024 * sub_ + 512 * half:
                             1024 * sub_ + 512 * (half + 1)],
                         acc[:, 1024 * sub_ + 512 * half:
                             1024 * sub_ + 512 * (half + 1)],
                         tmp_f[:]),
                     reads=[("acc",), ("tmp_f",)], writes=[("acc",)],
                     tag="sdacc")
        if ti == 0:
            p.dbg("accS", acc[:], [("acc",)], [128, 8192], F32)
        for sub_ in range(8):
            kt = 8 * tt + sub_
            xe = xep[sub_ % 2]
            xek = ("xin", sub_ % 2)
            p.dma("gpsimd", xe[:], x_dram[128 * kt:128 * (kt + 1), :],
                  writes=[xek], tag="xreload")
            p.op("vector",
                 lambda e, xe=xe, sub_=sub_: e.tensor_add(
                     acc[:, 1024 * sub_:1024 * (sub_ + 1)], xe[:],
                     acc[:, 1024 * sub_:1024 * (sub_ + 1)]),
                 reads=[xek, ("acc",)], writes=[("acc",)], tag="resid")
            emit_ln_epilogue(p, pool, ("acc",),
                             acc[:, 1024 * sub_:1024 * (sub_ + 1)],
                             out_dram[128 * kt:128 * (kt + 1), :],
                             ("out", kt), sq_t=xe, sq_k=xek,
                             yln_t=xe, yln_k=xek)


def _build_moe(tiles=range(4), debug=False):
    nc = bass.Bass("TRN2", target_bir_lowering=False)
    st = ExitStack()
    x_d = nc.dram_tensor("x", [T, D], F32, kind="ExternalInput").ap()
    wr_d = nc.dram_tensor("Wr", [1024, 8], F32, kind="ExternalInput").ap()
    wg_d = nc.dram_tensor("Wg", [8, 1024, 512], F32, kind="ExternalInput").ap()
    wu_d = nc.dram_tensor("Wu", [8, 1024, 512], F32, kind="ExternalInput").ap()
    wd_d = nc.dram_tensor("Wd", [8, 512, 1024], F32, kind="ExternalInput").ap()
    sg_d = nc.dram_tensor("Sg", [1024, 1024], F32, kind="ExternalInput").ap()
    su_d = nc.dram_tensor("Su", [1024, 1024], F32, kind="ExternalInput").ap()
    sd_d = nc.dram_tensor("Sd", [1024, 1024], F32, kind="ExternalInput").ap()
    eb_d = nc.dram_tensor("ebias", [8], F32, kind="ExternalInput").ap()
    out_d = nc.dram_tensor("out", [T, D], F32, kind="ExternalOutput").ap()
    p = Prog(nc)
    p.debug = debug
    pool = Pool(nc, st)
    for r in ("pA", "pB", "pC", "pD", "pE", "pF", "pT"):
        p.psum_keys.add(("ps", r))
    C = _declare_consts(nc, pool, p,
                        needed={"IDENT", "ONESBF", "IOTA9", "IOTA"})
    p.warm_src = x_d[0:1, 0:32]
    emit_moe(p, pool, C, x_d, out_d, wr_d, wg_d, wu_d, wd_d, sg_d, su_d, sd_d,
             eb_d, tiles=tiles)
    p.finalize_and_emit(st)
    st.close()
    return nc


def run_moe(x_full, wr, wg, wu, wd, sg, su, sd, eb, tiles=range(4),
            debug=False, ncore=NCORE):
    nc = _build_moe(tiles=tiles, debug=debug)
    hc = _host_consts()
    hc = {k: hc[k] for k in ("IDENT", "ONESBF", "IOTA9", "IOTA")}
    in_maps = []
    for b in range(ncore):
        m = {"x": np.ascontiguousarray(x_full[b]), "Wr": wr, "Wg": wg, "Wu": wu,
             "Wd": wd, "Sg": sg, "Su": su, "Sd": sd, "ebias": eb}
        m.update(hc)
        in_maps.append(m)
    res = run_bass_kernel_spmd(nc, in_maps, core_ids=list(range(ncore)))
    if debug:
        return res
    return np.stack([r["out"] for r in res.results])


def emit_l1_attn(p, pool, C, x_dram, out_dram, wq_d, wk_d, wv_d, wo_d, bo_d,
                 qgroups=range(8)):
    nc = p.nc
    H = 16
    wqb = pool.sb("wqb", 8192, BF16)
    load_weight_bf16(p, pool, wqb, ("wqb",), wq_d, 1024, 1024, "stage",
                     stage_cols=2048)
    wkb = pool.sb("wkb", 2048, BF16)
    load_weight_bf16(p, pool, wkb, ("wkb",), wk_d, 1024, 256, "stage",
                     stage_cols=2048)
    wvb = pool.sb("wvb", 2048, BF16)
    load_weight_bf16(p, pool, wvb, ("wvb",), wv_d, 1024, 256, "stage",
                     stage_cols=2048)
    wob = pool.sb("wob", 8192, BF16)
    load_weight_bf16(p, pool, wob, ("wob",), wo_d, 1024, 1024, "stage",
                     stage_cols=2048)
    bob = pool.sb("biasrow", 1024, BF16, parts=1)
    bstg = pool.sb("stage", 2048, F32)
    p.dma("gpsimd", bstg[0:1, 0:1024], bo_d.rearrange("(a n) -> a n", a=1),
          writes=[("stage",)])
    p.op("vector", lambda e: e.tensor_copy(bob[0:1, :], bstg[0:1, 0:1024]),
         reads=[("stage",)], writes=[("biasrow",)])

    biasc = pool.sb("biasc", 1, F32)
    epsl = pool.sb("epsl", 1, F32)
    if not getattr(pool, "_attc_done", False):
        pool._attc_done = True
        p.op("vector", lambda e: e.memset(biasc[:], -25.0), writes=[("biasc",)])
        p.op("vector", lambda e: e.memset(epsl[:], 1.0e-26), writes=[("epsl",)])

    xfm = pool.sb("xfmS", 4096, BF16)
    kfmL = pool.sb("kfmL", 16384, BF16)
    vauL = pool.sb("vauL", 16384, BF16)
    qfm = pool.sb("qfm", 4096, BF16)
    ctx = pool.sb("ctxfm", 4096, BF16)
    pbf = [pool.sb(f"pbf{i}", 512, BF16) for i in range(2)]
    lnl = pool.sb("lnl", 512, F32, parts=64)
    rbf = pool.sb("rbf", 512, F32, parts=64)
    cxf = pool.sb("cxf", 512, F32, parts=64)
    att = pool.sb("att", 1024, F32)
    yy = pool.sb("yy", 1024, F32)
    xq = [pool.sb(f"xin{i}", 1024, F32) for i in range(4)]

    pS = [pool.ps("pA"), pool.ps("pB")]
    pC_ = pool.ps("pC")
    pM = [pool.ps("pD"), pool.ps("pE")]
    ones_bf = C["ONESBF"]

    for kg in range(8):
        for sub_ in range(4):
            kt = 4 * kg + sub_
            xt = xq[sub_]
            xk = ("xin", sub_)
            p.dma("gpsimd", xt[:], x_dram[128 * kt:128 * (kt + 1), :],
                  writes=[xk], tag="kxload")
            emit_cast_transpose_tile(p, pool, C, xt[:], xk, xfm, ("xfmS",),
                                     512, 128 * sub_, sub_ % 2)
        for kc in range(2):
            pp = pM[kc]
            ppk = ("ps", "pD" if kc == 0 else "pE")
            for ic in range(8):
                p.op("tensor",
                     lambda e, kc=kc, ic=ic, pp=pp: e.matmul(
                         pp[:], wkb[:, 256 * ic + 128 * kc:256 * ic + 128 * (kc + 1)],
                         xfm[:, 512 * ic:512 * (ic + 1)],
                         start=(ic == 0), stop=(ic == 7)),
                     reads=[("wkb",), ("xfmS",)], writes=[ppk], tag="Kmm")
            for gg in range(2):
                g_ = 2 * kc + gg
                for dsth in range(2):
                    p.op("scalar",
                         lambda e, pp=pp, gg=gg, g_=g_, dsth=dsth, kg=kg:
                         e.copy(kfmL[64 * dsth:64 * (dsth + 1),
                                     4096 * g_ + 512 * kg:4096 * g_ + 512 * (kg + 1)],
                                pp[64 * gg:64 * (gg + 1), :]),
                         reads=[ppk], writes=[("kfmL",)], tag="Kevac")
        for sub_ in range(4):
            kt = 4 * kg + sub_
            pp = pM[0]
            ppk = ("ps", "pD")
            for ic in range(8):
                p.op("tensor",
                     lambda e, sub_=sub_, ic=ic, pp=pp: e.matmul(
                         pp[:, 0:256],
                         xfm[:, 512 * ic + 128 * sub_:512 * ic + 128 * (sub_ + 1)],
                         wvb[:, 256 * ic:256 * (ic + 1)],
                         start=(ic == 0), stop=(ic == 7)),
                     reads=[("wvb",), ("xfmS",)], writes=[ppk], tag="Vmm")
            p.op("vector",
                 lambda e, kt=kt, pp=pp: e.tensor_copy(
                     vauL[:, 512 * kt:512 * (kt + 1)]
                     .rearrange("p (g d) -> p g d", d=128)[:, :, 0:64],
                     pp[:, 0:256].rearrange("p (g d) -> p g d", d=64)),
                 reads=[ppk], writes=[("vauL",)], tag="Vevac")
            p.op("gpsimd",
                 lambda e, kt=kt: e.memset(
                     vauL[:, 512 * kt:512 * (kt + 1)]
                     .rearrange("p (g d) -> p g d", d=128)[:, :, 64:128], 1.0),
                 writes=[("vauL",)], tag="Vones")

    for qg in qgroups:
        for sub_ in range(4):
            qt = 4 * qg + sub_
            xt = xq[sub_]
            xk = ("xin", sub_)
            p.dma("gpsimd", xt[:], x_dram[128 * qt:128 * (qt + 1), :],
                  writes=[xk], tag="qxload")
            emit_cast_transpose_tile(p, pool, C, xt[:], xk, xfm, ("xfmS",),
                                     512, 128 * sub_, sub_ % 2)
        for qc in range(8):
            pp = pM[qc % 2]
            ppk = ("ps", "pD" if qc % 2 == 0 else "pE")
            for ic in range(8):
                p.op("tensor",
                     lambda e, qc=qc, ic=ic, pp=pp: e.matmul(
                         pp[:], wqb[:, 1024 * ic + 128 * qc:1024 * ic + 128 * qc + 128],
                         xfm[:, 512 * ic:512 * (ic + 1)],
                         start=(ic == 0), stop=(ic == 7)),
                     reads=[("wqb",), ("xfmS",)], writes=[ppk], tag="Qmm")
            p.op("scalar",
                 lambda e, qc=qc, pp=pp: e.copy(qfm[:, 512 * qc:512 * (qc + 1)],
                                                pp[:]),
                 reads=[ppk], writes=[("qfm",)], tag="Qevac")
        for h in range(H):
            g = h // 4
            kc, kbase = g // 2, 64 * (g % 2)
            qc, qbase = h // 2, 64 * (h % 2)
            nkt = 4 * qg + 4
            for kt in range(nkt):
                diag = kt >= 4 * qg
                o = 128 * (kt - 4 * qg) if diag else 0
                pp = pS[kt % 2]
                ppk = ("ps", "pA" if kt % 2 == 0 else "pB")
                p.op("tensor",
                     lambda e, pp=pp, g=g, qc=qc, qbase=qbase,
                     kt=kt, o=o: e.matmul(
                         pp[:, o:512],
                         kfmL[qbase:qbase + 64,
                              4096 * g + 128 * kt:4096 * g + 128 * (kt + 1)],
                         qfm[qbase:qbase + 64, 512 * qc + o:512 * (qc + 1)],
                         start=True, stop=True),
                     reads=[("kfmL",), ("qfm",)], writes=[ppk], tag="Smm")
                pb = pbf[kt % 2]
                pbk = ("pbf", kt % 2)
                p.op("scalar",
                     lambda e, pp=pp, pb=pb, o=o: e.activation(
                         pb[:, o:512], pp[:, o:512], AF.Exp,
                         bias=biasc[:, 0:1], scale=0.125),
                     reads=[ppk, ("biasc",)], writes=[pbk], tag="exp")
                if diag:
                    p.op("vector",
                         lambda e, pb=pb, o=o: e.tensor_mul(
                             pb[:, o:o + 128], pb[:, o:o + 128],
                             C["L1M"][:, 512:640]),
                         reads=[pbk, ("c", "L1M")], writes=[pbk], tag="maskmul")
                p.op("tensor",
                     lambda e, pb=pb, kt=kt, g=g, o=o, nkt=nkt: e.matmul(
                         pC_[:, o:512],
                         vauL[:, 512 * kt + 128 * g:512 * kt + 128 * (g + 1)],
                         pb[:, o:512], start=(kt == 0), stop=(kt == nkt - 1)),
                     reads=[pbk, ("vauL",)], writes=[("ps", "pC")], tag="PV")
            p.op("scalar",
                 lambda e: e.activation(lnl[0:64, :], pC_[64:128, :], AF.Ln,
                                        bias=epsl[0:64, 0:1]),
                 reads=[("ps", "pC"), ("epsl",)], writes=[("lnl",)], tag="lnl")
            p.op("scalar",
                 lambda e: e.activation(rbf[0:64, :], lnl[0:64, :], AF.Exp,
                                        scale=-1.0),
                 reads=[("lnl",)], writes=[("rbf",)], tag="rexp")
            p.op("scalar",
                 lambda e: e.copy(cxf[0:64, :], pC_[0:64, :]),
                 reads=[("ps", "pC")], writes=[("cxf",)], tag="ctxevac")
            p.op("vector",
                 lambda e, qc=qc, qbase=qbase: e.tensor_mul(
                     ctx[qbase:qbase + 64, 512 * qc:512 * (qc + 1)],
                     cxf[0:64, :], rbf[0:64, :]),
                 reads=[("cxf",), ("rbf",)], writes=[("ctx",)], tag="ctxmul")
        for qt in range(4):
            for half in range(2):
                pp = pM[half]
                ppk = ("ps", "pD" if half == 0 else "pE")
                for c in range(8):
                    p.op("tensor",
                         lambda e, pp=pp, c=c, qt=qt, half=half:
                         e.matmul(pp[:],
                                  ctx[:, 512 * c + 128 * qt:512 * c + 128 * (qt + 1)],
                                  wob[:, 1024 * c + 512 * half:
                                      1024 * c + 512 * (half + 1)],
                                  start=(c == 0), stop=False),
                         reads=[("ctx",), ("wob",)], writes=[ppk], tag="Omm")
                p.op("tensor",
                     lambda e, pp=pp, half=half: e.matmul(
                         pp[:], ones_bf[0:1, :],
                         bob[0:1, 512 * half:512 * (half + 1)],
                         start=False, stop=True),
                     reads=[("biasrow",), ("c", "ONESBF")], writes=[ppk],
                     tag="Obias")
                p.op("scalar",
                     lambda e, pp=pp, half=half: e.copy(
                         att[:, 512 * half:512 * (half + 1)], pp[:]),
                     reads=[ppk], writes=[("att",)], tag="Oevac")
            xt = xq[qt]
            xk = ("xin", qt)
            p.op("vector",
                 lambda e, xt=xt: e.tensor_add(yy[:], xt[:], att[:]),
                 reads=[xk, ("att",)], writes=[("yy",)], tag="resid")
            ktile = 4 * qg + qt
            emit_ln_epilogue(p, pool, ("yy",), yy,
                             out_dram[128 * ktile:128 * (ktile + 1), :],
                             ("out", ktile), sq_t=att, sq_k=("att",),
                             yln_t=xt, yln_k=xk)


def emit_l1_ffn(p, pool, C, x_dram, out_dram, wT_d, b_d, ld_d, lu_d, daux_d,
                uauxT_d):
    nc = p.nc
    lu = pool.sb("lu", 4, BF16, parts=50)
    ld = pool.sb("ld", 128, BF16, parts=4)
    da = pool.sb("da", 1024, BF16, parts=100)
    ua = pool.sb("ua", 1024, BF16, parts=50)
    stg = pool.sb("stage", 2048, F32)
    for nm, dram, dst, pr, cc in [("lu", lu_d, lu, 50, 4), ("ld", ld_d, ld, 4, 100),
                                  ("da", daux_d, da, 100, 1024),
                                  ("ua", uauxT_d, ua, 50, 1024)]:
        p.dma("gpsimd", stg[0:pr, 0:cc], dram, writes=[("stage",)], tag="lload")
        p.op("vector",
             lambda e, dst=dst, pr=pr, cc=cc: e.tensor_copy(dst[0:pr, 0:cc],
                                                            stg[0:pr, 0:cc]),
             reads=[("stage",)], writes=[(nm,)], tag="lcast")
    k1 = pool.sb("k1", 1024, BF16, parts=4)
    k2 = pool.sb("k2", 1024, BF16, parts=100)
    wc = pool.sb("wcb", 8192, BF16)
    att4 = pool.sb("att", 1024, F32)
    yy4 = pool.sb("yy", 1024, F32)
    bb = pool.sb("biasrow", 1024, BF16, parts=1)
    p.dma("gpsimd", stg[0:1, 0:1024], b_d.rearrange("(a n) -> a n", a=1),
          writes=[("stage",)])
    p.op("vector", lambda e: e.tensor_copy(bb[0:1, :], stg[0:1, 0:1024]),
         reads=[("stage",)], writes=[("biasrow",)])
    pM = [pool.ps("pD"), pool.ps("pE")]
    for half in range(2):
        p.op("tensor",
             lambda e, half=half: e.matmul(
                 pM[half][0:4, :], lu[0:50, 0:4], ua[0:50, 512 * half:512 * (half + 1)],
                 start=True, stop=True),
             reads=[("lu",), ("ua",)], writes=[("ps", "pD" if half == 0 else "pE")],
             tag="k1mm")
        p.op("vector",
             lambda e, half=half: e.tensor_copy(k1[0:4, 512 * half:512 * (half + 1)],
                                                pM[half][0:4, :]),
             reads=[("ps", "pD" if half == 0 else "pE")], writes=[("k1",)],
             tag="k1evac")
    for half in range(2):
        p.op("tensor",
             lambda e, half=half: e.matmul(
                 pM[half][0:100, :], ld[0:4, 0:100],
                 k1[0:4, 512 * half:512 * (half + 1)], start=True, stop=True),
             reads=[("ld",), ("k1",)], writes=[("ps", "pD" if half == 0 else "pE")],
             tag="k2mm")
        p.op("vector",
             lambda e, half=half: e.tensor_copy(k2[0:100, 512 * half:512 * (half + 1)],
                                                pM[half][0:100, :]),
             reads=[("ps", "pD" if half == 0 else "pE")], writes=[("k2",)],
             tag="k2evac")
    for c in range(8):
        for half in range(2):
            pp = pM[half]
            ppk = ("ps", "pD" if half == 0 else "pE")
            p.op("tensor",
                 lambda e, c=c, half=half, pp=pp: e.matmul(
                     pp[:], da[0:100, 128 * c:128 * (c + 1)],
                     k2[0:100, 512 * half:512 * (half + 1)],
                     start=True, stop=True),
                 reads=[("da",), ("k2",)], writes=[ppk], tag="Rmm2")
        p.dma("gpsimd",
              stg[:, 0:1024],
              wT_d[128 * c:128 * (c + 1), :], writes=[("stage",)], tag="wTload")
        for half in range(2):
            pp = pM[half]
            ppk = ("ps", "pD" if half == 0 else "pE")
            p.op("scalar",
                 lambda e, pp=pp, half=half: e.copy(
                     att4[:, 512 * half:512 * (half + 1)], pp[:]),
                 reads=[ppk], writes=[("att4",)], tag="Revac")
        p.op("vector",
             lambda e, c=c: e.tensor_add(wc[:, 1024 * c:1024 * (c + 1)],
                                         stg[:, 0:1024], att4[:, 0:1024]),
             reads=[("stage",), ("att4",)], writes=[("wcb",)], tag="Wcomb")
    xfm = pool.sb("xfmS", 4096, BF16)
    for tg in range(8):
        for sub_ in range(4):
            kt = 4 * tg + sub_
            xt = pool.sb(f"xin{sub_}", 1024, F32)
            xk = ("xin", sub_)
            p.dma("gpsimd", xt[:], x_dram[128 * kt:128 * (kt + 1), :],
                  writes=[xk], tag="fxload")
            emit_cast_transpose_tile(p, pool, C, xt[:], xk, xfm, ("xfmS",),
                                     512, 128 * sub_, sub_ % 2)
        for sub_ in range(4):
            kt = 4 * tg + sub_
            for half in range(2):
                pp = pM[half]
                ppk = ("ps", "pD" if half == 0 else "pE")
                for ic in range(8):
                    p.op("tensor",
                         lambda e, ic=ic, sub_=sub_, half=half, pp=pp:
                         e.matmul(pp[:],
                                  xfm[:, 512 * ic + 128 * sub_:512 * ic + 128 * (sub_ + 1)],
                                  wc[:, 1024 * ic + 512 * half:1024 * ic + 512 * (half + 1)],
                                  start=(ic == 0), stop=False),
                         reads=[("xfmS",), ("wcb",)], writes=[ppk], tag="Fmm")
                p.op("tensor",
                     lambda e, pp=pp, half=half: e.matmul(
                         pp[:], C["ONESBF"][0:1, :],
                         bb[0:1, 512 * half:512 * (half + 1)],
                         start=False, stop=True),
                     reads=[("biasrow",), ("c", "ONESBF")], writes=[ppk],
                     tag="Fbias")
                p.op("scalar",
                     lambda e, pp=pp, half=half: e.copy(
                         att4[:, 512 * half:512 * (half + 1)], pp[:]),
                     reads=[ppk], writes=[("att4",)], tag="Fevac")
            xt = pool.sb(f"xin{sub_}", 1024, F32)
            xk = ("xin", sub_)
            p.op("vector",
                 lambda e, xt=xt: e.tensor_add(yy4[:], xt[:], att4[:]),
                 reads=[xk, ("att4",)], writes=[("yy4",)], tag="resid")
            emit_ln_epilogue(p, pool, ("yy4",), yy4,
                             out_dram[128 * kt:128 * (kt + 1), :],
                             ("out", kt), sq_t=att4, sq_k=("att4",),
                             yln_t=xt, yln_k=xk)


def _build_l1_attn(qgroups=range(8), debug=False):
    nc = bass.Bass("TRN2", target_bir_lowering=False)
    st = ExitStack()
    x_d = nc.dram_tensor("x", [T, D], F32, kind="ExternalInput").ap()
    wq_d = nc.dram_tensor("Wq", [1024, 1024], F32, kind="ExternalInput").ap()
    wk_d = nc.dram_tensor("Wk", [1024, 256], F32, kind="ExternalInput").ap()
    wv_d = nc.dram_tensor("Wv", [1024, 256], F32, kind="ExternalInput").ap()
    wo_d = nc.dram_tensor("Wo", [1024, 1024], F32, kind="ExternalInput").ap()
    bo_d = nc.dram_tensor("bo", [1024], F32, kind="ExternalInput").ap()
    out_d = nc.dram_tensor("out", [T, D], F32, kind="ExternalOutput").ap()
    p = Prog(nc)
    p.debug = debug
    pool = Pool(nc, st)
    for r in ("pA", "pB", "pC", "pD", "pE", "pF", "pT"):
        p.psum_keys.add(("ps", r))
    C = _declare_consts(nc, pool, p, needed={"IDENT", "ONESBF", "L1M"})
    p.warm_src = x_d[0:1, 0:32]
    emit_l1_attn(p, pool, C, x_d, out_d, wq_d, wk_d, wv_d, wo_d, bo_d,
                 qgroups=qgroups)
    p.finalize_and_emit(st)
    st.close()
    return nc


def run_l1_attn(x_full, wq, wk, wv, wo, bo, qgroups=range(8), debug=False,
                ncore=NCORE):
    nc = _build_l1_attn(qgroups=qgroups, debug=debug)
    hc = _host_consts()
    hc = {k: hc[k] for k in ("IDENT", "ONESBF", "L1M")}
    in_maps = []
    for b in range(ncore):
        m = {"x": np.ascontiguousarray(x_full[b]), "Wq": wq, "Wk": wk,
             "Wv": wv, "Wo": wo, "bo": bo}
        m.update(hc)
        in_maps.append(m)
    res = run_bass_kernel_spmd(nc, in_maps, core_ids=list(range(ncore)))
    if debug:
        return res
    return np.stack([r["out"] for r in res.results])


def _build_l1_ffn(debug=False):
    nc = bass.Bass("TRN2", target_bir_lowering=False)
    st = ExitStack()
    x_d = nc.dram_tensor("x", [T, D], F32, kind="ExternalInput").ap()
    wT_d = nc.dram_tensor("WT", [1024, 1024], F32, kind="ExternalInput").ap()
    b_d = nc.dram_tensor("b", [1024], F32, kind="ExternalInput").ap()
    ld_d = nc.dram_tensor("lora_down", [4, 100], F32, kind="ExternalInput").ap()
    lu_d = nc.dram_tensor("lora_up", [50, 4], F32, kind="ExternalInput").ap()
    da_d = nc.dram_tensor("down_aux", [100, 1024], F32, kind="ExternalInput").ap()
    uaT_d = nc.dram_tensor("up_auxT", [50, 1024], F32, kind="ExternalInput").ap()
    out_d = nc.dram_tensor("out", [T, D], F32, kind="ExternalOutput").ap()
    p = Prog(nc)
    p.debug = debug
    pool = Pool(nc, st)
    for r in ("pA", "pB", "pC", "pD", "pE", "pF", "pT"):
        p.psum_keys.add(("ps", r))
    C = _declare_consts(nc, pool, p, needed={"IDENT", "ONESBF"})
    p.warm_src = x_d[0:1, 0:32]
    emit_l1_ffn(p, pool, C, x_d, out_d, wT_d, b_d, ld_d, lu_d, da_d, uaT_d)
    p.finalize_and_emit(st)
    st.close()
    return nc


def run_l1_ffn(x_full, W, b, lora_down, lora_up, down_aux, up_aux,
               debug=False, ncore=NCORE):
    nc = _build_l1_ffn(debug=debug)
    hc = _host_consts()
    hc = {k: hc[k] for k in ("IDENT", "ONESBF")}
    wT = np.ascontiguousarray(W.T)
    uaT = np.ascontiguousarray(up_aux.T)
    in_maps = []
    for b_ in range(ncore):
        m = {"x": np.ascontiguousarray(x_full[b_]), "WT": wT, "b": b,
             "lora_down": lora_down, "lora_up": lora_up, "down_aux": down_aux,
             "up_auxT": uaT}
        m.update(hc)
        in_maps.append(m)
    res = run_bass_kernel_spmd(nc, in_maps, core_ids=list(range(ncore)))
    if debug:
        return res
    return np.stack([r["out"] for r in res.results])


def _build_l0_attn(groups=range(8), debug=False):
    nc = bass.Bass("TRN2", target_bir_lowering=False)
    st = ExitStack()
    x_d = nc.dram_tensor("x", [T, D], F32, kind="ExternalInput").ap()
    wqkv_d = nc.dram_tensor("w_qkv", [1024, 3072], F32, kind="ExternalInput").ap()
    wproj_d = nc.dram_tensor("w_proj", [1024, 1024], F32, kind="ExternalInput").ap()
    bproj_d = nc.dram_tensor("b_proj", [1024], F32, kind="ExternalInput").ap()
    out_d = nc.dram_tensor("out", [T, D], F32, kind="ExternalOutput").ap()
    p = Prog(nc)
    p.debug = debug
    pool = Pool(nc, st)
    for r in ("pA", "pB", "pC", "pD", "pE", "pF"):
        k = ("ps", r)
        p.psum_keys.add(k)
    p.psum_keys.add(("ps", "pT"))
    C = _declare_consts(nc, pool, p)
    p.warm_src = x_d[0:1, 0:32]
    emit_l0_attn(p, pool, C, x_d, out_d, wqkv_d, wproj_d, bproj_d, groups=groups)
    p.finalize_and_emit(st)
    st.close()
    return nc


def run_l0_attn(x_full, w_qkv, w_proj, b_proj, groups=range(8), debug=False,
                ncore=NCORE):
    nc = _build_l0_attn(groups=groups, debug=debug)
    hc = _host_consts()
    in_maps = []
    for b in range(ncore):
        m = {"x": np.ascontiguousarray(x_full[b]),
             "w_qkv": w_qkv, "w_proj": w_proj, "b_proj": b_proj}
        m.update(hc)
        in_maps.append(m)
    res = run_bass_kernel_spmd(nc, in_maps, core_ids=list(range(ncore)))
    if debug:
        return res
    return np.stack([r["out"] for r in res.results])


def kernel(**inputs):
    x1 = run_l0_attn(inputs["L0_attn_x"], inputs["L0_attn_w_qkv"],
                     inputs["L0_attn_w_proj"], inputs["L0_attn_b_proj"])
    x2 = run_moe(x1, inputs["L0_ffn_Wr"], inputs["L0_ffn_Wg"],
                 inputs["L0_ffn_Wu"], inputs["L0_ffn_Wd"], inputs["L0_ffn_Sg"],
                 inputs["L0_ffn_Su"], inputs["L0_ffn_Sd"],
                 inputs["L0_ffn_expert_bias"])
    x3 = run_l1_attn(x2, inputs["L1_attn_Wq"], inputs["L1_attn_Wk"],
                     inputs["L1_attn_Wv"], inputs["L1_attn_Wo"],
                     inputs["L1_attn_bo"])
    out = run_l1_ffn(x3, inputs["L1_ffn_W"], inputs["L1_ffn_b"],
                     inputs["L1_ffn_lora_down"], inputs["L1_ffn_lora_up"],
                     inputs["L1_ffn_down_aux"], inputs["L1_ffn_up_aux"])
    return out


# version 04 (final)
# speedup vs baseline: 1.0627x; 1.0627x over previous
import numpy as np
import ml_dtypes
from contextlib import ExitStack

import concourse.bass as bass
import concourse.mybir as mybir
from concourse.bass_utils import run_bass_kernel_spmd

F32 = mybir.dt.float32
BF16 = mybir.dt.bfloat16
AF = mybir.ActivationFunctionType
OP = mybir.AluOpType
AX = mybir.AxisListType
bf16 = ml_dtypes.bfloat16

NCORE = 8
T = 4096
D = 1024
NT = T // 128

ENGS = ("tensor", "vector", "scalar", "gpsimd", "sync")


class Prog:
    def __init__(self, nc):
        self.nc = nc
        self.ops = []
        self.lastw = {}
        self.readers = {}
        self.psum_keys = set()

    def op(self, eng, fn, reads=(), writes=(), dma=False, tag=""):
        deps = set()
        for k in reads:
            i = self.lastw.get(k)
            if i is not None:
                deps.add(i)
            if k in self.psum_keys:
                for ridx, reng in self.readers.get(k, {}).items():
                    if reng != eng:
                        deps.add(ridx)
        for k in writes:
            i = self.lastw.get(k)
            if i is not None:
                deps.add(i)
            deps.update(self.readers.get(k, {}).keys())
        idx = len(self.ops)
        self.ops.append(dict(eng=eng, fn=fn, deps=deps, dma=dma, tag=tag,
                             rd=frozenset(reads), wr=frozenset(writes)))
        for k in reads:
            self.readers.setdefault(k, {})[idx] = eng
        for k in writes:
            self.lastw[k] = idx
            self.readers[k] = {}
        return idx

    def dbg(self, name, src_ap, reads, shape, dt):
        if not getattr(self, "debug", False):
            return
        d = self.nc.dram_tensor("dbg_" + name, list(shape), dt,
                                kind="ExternalOutput").ap()
        self.dma("sync", d, src_ap, reads=reads, tag="dbg_" + name)

    def dma(self, queue, dst, src, reads=(), writes=(), tag=""):
        fn = lambda e: e.dma_start(dst, src)
        return self.op(queue, fn, reads=reads, writes=writes, dma=True, tag=tag)

    def finalize_and_emit(self, st, warm=True):
        nc = self.nc
        n = len(self.ops)
        eng_of = [o["eng"] for o in self.ops]
        edges = []
        for i, o in enumerate(self.ops):
            for d in o["deps"]:
                if eng_of[d] != o["eng"] or self.ops[d]["dma"]:
                    edges.append((d, i))
        needs_inc = set()
        cons_of = {}
        for d, i in edges:
            cons_of.setdefault(i, []).append(d)
            if not self.ops[d]["dma"]:
                needs_inc.add(d)
        NLANE = 6
        inc_idx = {}
        eng_count = {e: 0 for e in ENGS}
        dma_sem = {}
        dma_idx = {}
        lane_count = {}
        dma_seq = {e: 0 for e in ENGS}
        for i, o in enumerate(self.ops):
            if o["dma"]:
                e = o["eng"]
                lane = dma_seq[e] % NLANE
                dma_seq[e] += 1
                key = (e, lane)
                lane_count[key] = lane_count.get(key, 0) + 1
                dma_sem[i] = f"dma_{e}_{lane}"
                dma_idx[i] = 16 * lane_count[key]
            elif i in needs_inc:
                eng_count[o["eng"]] += 1
                inc_idx[i] = eng_count[o["eng"]]
        sems = {}
        fid = getattr(self, "_flush_id", 0)
        self._flush_id = fid + 1
        for e in ENGS:
            if eng_count[e] > 0:
                sems[e] = st.enter_context(nc.semaphore(f"sem{fid}_{e}"))
        for (e, lane), cnt in lane_count.items():
            sems[f"dma_{e}_{lane}"] = st.enter_context(
                nc.semaphore(f"dsem{fid}_{e}_{lane}"))
        waited = {e: {} for e in ENGS}
        waits_for = {}
        for i, o in enumerate(self.ops):
            prods = cons_of.get(i, ())
            req = {}
            for d in prods:
                if self.ops[d]["dma"]:
                    sname, val = dma_sem[d], dma_idx[d]
                else:
                    sname, val = eng_of[d], inc_idx[d]
                if req.get(sname, -1) < val:
                    req[sname] = val
            if o["dma"] and dma_idx[i] > 16:
                sname = dma_sem[i]
                if req.get(sname, -1) < dma_idx[i] - 16:
                    req[sname] = dma_idx[i] - 16
            wl = []
            wd = waited[o["eng"]]
            for sname, val in req.items():
                if wd.get(sname, -1) < val:
                    wd[sname] = val
                    wl.append((sname, val))
            if wl:
                waits_for[i] = wl
        self._sems = sems
        self._trace = (inc_idx, dma_idx, waits_for)

        for sname in sorted(sems):
            nc.sync.sem_clear(sems[sname])
        nc.all_engine_barrier()
        if warm:
            wid = getattr(self, "_warm_id", 0)
            self._warm_id = wid + 1
            with nc.sbuf_tensor(f"act_warm{wid}", [128, 32], F32) as warmt:
                for fn_ in getattr(self, "act_warm",
                                   (AF.Exp, AF.Ln, AF.Square, AF.Silu, AF.Tanh)):
                    nc.scalar.activation(warmt[:], warmt[:], fn_)
            nwarm = getattr(self, "warmup_dmas", 101)
            if nwarm and getattr(self, "warm_src", None) is not None:
                with (nc.sbuf_tensor(f"dma_warm{wid}", [1, 32], F32) as wtile,
                      nc.semaphore("warm_sem") as wsem):
                    nc.sync.sem_clear(wsem)
                    for wi in range(nwarm):
                        nc.sync.dma_start(wtile[0:1, :],
                                          self.warm_src).then_inc(wsem, 16)
                        nc.sync.wait_ge(wsem, 16 * (wi + 1))
            nc.all_engine_barrier()

        engs = {e: getattr(nc, e) for e in ENGS}
        hzid = getattr(self, "_hz_id", 0)
        self._hz_id = hzid + 1
        dummy_a = st.enter_context(nc.sbuf_tensor(f"hz_a{hzid}", [128, 2], F32))
        dummy_b = st.enter_context(nc.sbuf_tensor(f"hz_b{hzid}", [128, 2], F32))
        prev_w = {e: frozenset() for e in ENGS}
        for i, o in enumerate(self.ops):
            engobj = engs[o["eng"]]
            if (o["eng"] in ("vector", "gpsimd") and not o["dma"]
                    and (o["rd"] & prev_w[o["eng"]])):
                engobj.tensor_copy(dummy_a[:, 0:1], dummy_b[:, 0:1])
            prev_w[o["eng"]] = o["wr"] if not o["dma"] else frozenset()
            for sname, val in waits_for.get(i, ()):
                engobj.wait_ge(sems[sname], val)
            ins = o["fn"](engobj)
            if o["dma"]:
                ins.then_inc(sems[dma_sem[i]], 16)
            elif i in inc_idx:
                ins.then_inc(sems[o["eng"]], 1)
        for (e, lane), cnt in lane_count.items():
            engs[e].wait_ge(sems[f"dma_{e}_{lane}"], 16 * cnt)
        nc.all_engine_barrier()
        self.ops = []
        self.lastw = {}
        self.readers = {}


class Pool:
    def __init__(self, nc, st, prefix=""):
        self.nc, self.st = nc, st
        self.prefix = prefix
        self.sbufs = {}
        self.psums = {}

    def sb(self, role, cols, dt=F32, parts=128):
        key = (role, dt)
        if key not in self.sbufs:
            self.sbufs[key] = self.st.enter_context(
                self.nc.sbuf_tensor(f"sb_{self.prefix}{role}", [parts, cols], dt))
        t = self.sbufs[key]
        assert t.shape[1] >= cols, (role, t.shape, cols)
        return t

    def ps(self, role, cols=512, dt=F32):
        key = role
        if key not in self.psums:
            self.psums[key] = self.st.enter_context(
                self.nc.psum_tensor(f"ps_{self.prefix}{role}", [128, cols], dt))
        return self.psums[key]


def _host_consts():
    c = {}
    c["IDENT"] = np.eye(128, dtype=bf16)
    kr = np.arange(128)[:, None]
    col = np.arange(1152)[None, :]
    c["L0M"] = (np.abs(col - 512 - kr) <= 128).astype(bf16)
    col2 = np.arange(1024)[None, :]
    c["L1M"] = ((col2 - 512) >= kr).astype(bf16)
    c["ONESBF"] = np.ones((1, 128), dtype=bf16)
    c["ONESF"] = np.ones((1, 128), dtype=np.float32)
    c["ONESFC"] = np.ones((128, 1), dtype=np.float32)
    c["ONESBC"] = np.ones((128, 1), dtype=bf16)
    c["ONESALL"] = np.ones((128, 128), dtype=bf16)
    kk_ = np.arange(128)[:, None] // 64
    hh_ = np.arange(16)[None, :] % 2
    c["HSEL"] = (kk_ == hh_).astype(np.float32)
    c["IOTA9"] = ((np.arange(8, dtype=np.float32)[None, :] + 16.0)
                  * np.ones((128, 1), np.float32))
    c["IOTA"] = (np.arange(8, dtype=np.float32)[None, :]
                 * np.ones((128, 1), np.float32))
    return c


def _declare_consts(nc, pool, p, needed=None):
    aps = {}
    specs = {"IDENT": ([128, 128], BF16), "L0M": ([128, 1152], BF16),
             "L1M": ([128, 1024], BF16), "ONESBF": ([1, 128], BF16),
             "ONESF": ([1, 128], F32), "ONESFC": ([128, 1], F32),
             "ONESBC": ([128, 1], BF16), "ONESALL": ([128, 128], BF16),
             "HSEL": ([128, 16], F32),
             "IOTA9": ([128, 8], F32),
             "IOTA": ([128, 8], F32)}
    if needed is not None:
        specs = {k: v for k, v in specs.items() if k in needed}
    sb = {}
    for name, (shape, dt) in specs.items():
        aps[name] = nc.dram_tensor(name, shape, dt, kind="ExternalInput").ap()
        sb[name] = pool.sb("c_" + name, shape[1], dt, parts=shape[0])
        p.dma("sync", sb[name][:], aps[name], writes=[("c", name)], tag="const")
    return sb


def emit_ln_epilogue(p, pool, y_key, y, out_ap, out_key, dma_queue="sync",
                     sq_t=None, sq_k=None, yln_t=None, yln_k=None):
    nc = p.nc
    smA = pool.sb("lnA", 1, F32)
    smB = pool.sb("lnB", 1, F32)
    smC = pool.sb("lnC", 1, F32)
    smD = pool.sb("lnD", 1, F32)
    smE = pool.sb("lnE", 1, F32)
    smG = pool.sb("lnG", 1, F32)
    smF = pool.sb("lnF", 1, F32)
    eps = pool.sb("ln_eps", 1, F32)
    smk = ("ln_small",)
    if not getattr(pool, "_eps_done", False):
        pool._eps_done = True
        p.op("vector", lambda e: e.memset(eps[:], 1.0e-5), writes=[("ln_eps",)])
    sq = pool.sb("ln_sq", 1024, F32) if sq_t is None else sq_t
    sqk = ("ln_sq",) if sq_k is None else sq_k
    yln = pool.sb("ln_out", 1024, F32) if yln_t is None else yln_t
    ylnk = ("ln_out",) if yln_k is None else yln_k
    y_ap = y if isinstance(y, bass.AP) else y[:]
    p.op("vector", lambda e: e.reduce_sum(smA[:], y_ap, AX.X),
         reads=[y_key], writes=[smk], tag="ln_sum")
    p.op("scalar", lambda e: e.activation(sq[:], y_ap, AF.Square),
         reads=[y_key], writes=[sqk], tag="ln_sq")
    p.op("vector", lambda e: e.reduce_sum(smC[:], sq[:], AX.X),
         reads=[sqk], writes=[smk], tag="ln_ssq")
    p.op("vector", lambda e: e.tensor_scalar(smB[:], smA[:], 1.0 / D,
                                             None, OP.mult),
         reads=[smk], writes=[smk], tag="ln_mean")
    p.op("scalar", lambda e: e.activation(smD[:], smB[:], AF.Square),
         reads=[smk], writes=[smk], tag="ln_msq")
    p.op("vector", lambda e: e.scalar_tensor_tensor(smE[:], smC[:],
                                                    1.0 / D, smD[:],
                                                    OP.mult, OP.subtract),
         reads=[smk], writes=[smk], tag="ln_var")
    p.op("scalar", lambda e: e.activation(smF[:], smE[:], AF.Ln,
                                          bias=eps[:, 0:1]),
         reads=[smk, ("ln_eps",)], writes=[smk], tag="ln_lv")
    p.op("scalar", lambda e: e.activation(smG[:], smF[:], AF.Exp,
                                          scale=-0.5),
         reads=[smk], writes=[smk], tag="ln_rstd")
    p.op("vector", lambda e: e.tensor_scalar(yln[:], y_ap, smB[:, 0:1],
                                             smG[:, 0:1], OP.subtract, OP.mult),
         reads=[y_key, smk], writes=[ylnk], tag="ln_norm")
    if getattr(p, "debug", False) and not getattr(p, "_ln_dbg_done", False):
        p._ln_dbg_done = True
        p.dbg("ln_yy", y[:], [y_key], [128, 1024], F32)
        p.dbg("ln_sq", sq[:], [sqk], [128, 1024], F32)
        p.dbg("ln_smA", smA[:], [smk], [128, 1], F32)
        p.dbg("ln_smB", smB[:], [smk], [128, 1], F32)
        p.dbg("ln_smC", smC[:], [smk], [128, 1], F32)
        p.dbg("ln_smD", smD[:], [smk], [128, 1], F32)
        p.dbg("ln_smE", smE[:], [smk], [128, 1], F32)
        p.dbg("ln_smG", smG[:], [smk], [128, 1], F32)
        p.dbg("ln_yln", yln[:], [ylnk], [128, 1024], F32)
    p.dma(dma_queue, out_ap, yln[:], reads=[ylnk], writes=[out_key],
          tag="ln_out_dma")


def load_weight_bf16(p, pool, dst, dst_key, dram_ap, rows, cols, role_stage,
                     queue="gpsimd", stage_cols=4096):
    nch = rows // 128
    stg = pool.sb(role_stage, stage_cols, F32)
    stg_k = (role_stage,)
    per = max(1, stage_cols // cols)
    c = 0
    while c < nch:
        cn = min(per, nch - c)
        src = dram_ap.rearrange("(c p) n -> p c n", p=128)[:, c:c + cn, :]
        p.dma(queue, stg[:, 0:cn * cols].rearrange("p (c n) -> p c n", n=cols),
              src, writes=[stg_k], tag="wload")
        p.op("vector",
             lambda e, c=c, cn=cn: e.tensor_copy(
                 dst[:, cols * c:cols * (c + cn)], stg[:, 0:cols * cn]),
             reads=[stg_k], writes=[dst_key], tag="wcast")
        c += cn


def emit_l0_attn(p, pool, C, x_dram, out_dram, w_qkv_d, w_proj_d, b_proj_d, groups=range(8)):
    nc = p.nc
    H, Dh = 16, 64
    wqkv = pool.sb("wbig", 8 * 3072, BF16)
    load_weight_bf16(p, pool, wqkv, ("wbig",), w_qkv_d, 1024, 3072, "stage")
    wproj = pool.sb("bf8kA", 8192, BF16)
    load_weight_bf16(p, pool, wproj, ("bf8kA",), w_proj_d, 1024, 1024, "stage")
    bproj = pool.sb("biasrow", 1024, BF16, parts=1)
    bstg = pool.sb("stage", 4096, F32)
    p.dma("gpsimd", bstg[0:1, 0:1024], b_proj_d.rearrange("(a n) -> a n", a=1),
          writes=[("stage",)])
    p.op("vector", lambda e: e.tensor_copy(bproj[0:1, :], bstg[0:1, 0:1024]),
         reads=[("stage",)], writes=[("biasrow",)])

    biasc = pool.sb("biasc", 1, F32)
    epsl = pool.sb("epsl", 1, F32)
    if not getattr(pool, "_attc_done", False):
        pool._attc_done = True
        p.op("vector", lambda e: e.memset(biasc[:], -25.0), writes=[("biasc",)])
        p.op("vector", lambda e: e.memset(epsl[:], 1.0e-26), writes=[("epsl",)])
    xfm = pool.sb("xfm6", 6 * 1024, BF16)
    kfm = pool.sb("kfm6", 6 * 1024, BF16)
    qfm = pool.sb("qfm", 4096, BF16)
    vau = [pool.sb(f"vau{s}", 2048, BF16) for s in range(6)]
    ctx = pool.sb("ctxfm", 4096, BF16)
    lnl = pool.sb("lnl", 512, F32, parts=64)
    rbf = pool.sb("rbf", 512, F32, parts=64)
    cxf = pool.sb("cxf", 512, F32, parts=64)
    pbf = [pool.sb(f"pbf{i}", 512, BF16) for i in range(2)]
    att = pool.sb("att", 1024, F32)
    yy = pool.sb("yy", 1024, F32)

    pS = [pool.ps("pA"), pool.ps("pB")]
    pC_ = pool.ps("pC")
    pM = [pool.ps("pD"), pool.ps("pE")]
    pX = pool.ps("pF")
    ones_bf = C["ONESBF"]
    ones_f = C["ONESF"]

    for g in groups:
        kt0 = 4 * g - 1
        slots = [s for s in range(6) if 0 <= kt0 + s < 32]
        xin = {}
        tiles = []
        for s in slots:
            kt = kt0 + s
            xt = pool.sb(f"xin{s}", 1024, F32)
            xk = ("xin", s)
            p.dma("gpsimd", xt[:], x_dram[128 * kt:128 * (kt + 1), :],
                  writes=[xk], tag=f"xload{g}")
            xin[s] = (xt, xk)
            tiles.append((s, (xt[:], xk)))
        xfm_k = ("xfm",)
        pT = pool.ps("pT", 1024, BF16)
        for s_missing in [s for s in range(6) if s not in slots]:
            p.op("vector",
                 lambda e, s=s_missing: e.memset(
                     xfm[:].rearrange("p (c q) -> p c q", q=768)
                     [:, :, 128 * s:128 * (s + 1)], 0.0),
                 writes=[xfm_k], tag="xzero")
        for s, (xt_ap, xk) in tiles:
            xb = pool.sb(f"xbf{s % 2}", 1024, BF16)
            xbk = ("xbf", s % 2)
            p.op("vector", lambda e, o=xb, i=xt_ap: e.tensor_copy(o[:], i),
                 reads=[xk], writes=[xbk], tag="xcast")
            for c in range(8):
                p.op("tensor",
                     lambda e, o=pT, i=xb, c=c: e.transpose(
                         o[:, 128 * c:128 * (c + 1)],
                         i[:, 128 * c:128 * (c + 1)], C["IDENT"][:]),
                     reads=[xbk, ("c", "IDENT")], writes=[("ps", "pT")])
            p.op("scalar",
                 lambda e, s=s: e.copy(
                     xfm[:].rearrange("p (c q) -> p c q", c=6 + 2)[:, :, 0:0]
                     if False else
                     xfm[:].rearrange("p (c q) -> p c q", q=768)
                     [:, :, 128 * s:128 * (s + 1)],
                     pT[:].rearrange("p (c q) -> p c q", q=128)),
                 reads=[("ps", "pT")], writes=[xfm_k], tag="xevac")

        if g == 0:
            p.dbg("xfm", xfm[:], [xfm_k], [128, 6144], BF16)
        smin, smax = slots[0], slots[-1]
        for c in range(8):
            for half, (n0, nn) in enumerate([(0, 512), (512, 256)]):
                if 128 * smin >= n0 + nn or 128 * (smax + 1) <= n0:
                    pass
                pp = pM[half]
                ppk = ("ps", "pD" if half == 0 else "pE")
                for ic in range(8):
                    p.op("tensor",
                         lambda e, pp=pp, c=c, ic=ic, n0=n0, nn=nn:
                         e.matmul(pp[:, 0:nn],
                                  wqkv[:, 3072 * ic + 1024 + 128 * c:
                                       3072 * ic + 1024 + 128 * (c + 1)],
                                  xfm[:, 768 * ic + n0:768 * ic + n0 + nn],
                                  start=(ic == 0), stop=(ic == 7)),
                         reads=[("wbig",), xfm_k], writes=[ppk], tag="Kmm")
                p.op("scalar",
                     lambda e, c=c, n0=n0, nn=nn, pp=pp:
                     e.copy(kfm[:, 768 * c + n0:768 * c + n0 + nn], pp[:, 0:nn]),
                     reads=[ppk], writes=[("kfm",)], tag="Kevac")
        for s in slots:
            for half in range(2):
                pp = pM[half]
                ppk = ("ps", "pD" if half == 0 else "pE")
                for ic in range(8):
                    p.op("tensor",
                         lambda e, pp=pp, s=s, ic=ic, half=half:
                         e.matmul(pp[:],
                                  xfm[:, 768 * ic + 128 * s:768 * ic + 128 * (s + 1)],
                                  wqkv[:, 3072 * ic + 2048 + 512 * half:
                                       3072 * ic + 2048 + 512 * (half + 1)],
                                  start=(ic == 0), stop=(ic == 7)),
                         reads=[("wbig",), xfm_k], writes=[ppk], tag="Vmm")
                p.op("vector",
                     lambda e, s=s, half=half, pp=pp:
                     e.tensor_copy(
                         vau[s][:].rearrange("p (h d) -> p h d", d=128)
                         [:, 8 * half:8 * (half + 1), 0:64],
                         pp[:].rearrange("p (h d) -> p h d", d=64)),
                     reads=[ppk], writes=[("vau", s)], tag="Vevac")
            p.op("gpsimd",
                 lambda e, s=s: e.memset(
                     vau[s][:].rearrange("p (h d) -> p h d", d=128)[:, :, 64:128], 1.0),
                 writes=[("vau", s)], tag="Vones")
        for c in range(8):
            pp = pM[0]
            ppk = ("ps", "pD")
            for ic in range(8):
                p.op("tensor",
                     lambda e, c=c, ic=ic, pp=pp:
                     e.matmul(pp[:],
                              wqkv[:, 3072 * ic + 128 * c:3072 * ic + 128 * (c + 1)],
                              xfm[:, 768 * ic + 128:768 * ic + 640],
                              start=(ic == 0), stop=(ic == 7)),
                     reads=[("wbig",), xfm_k], writes=[ppk], tag="Qmm")
            p.op("scalar",
                 lambda e, c=c, pp=pp: e.copy(qfm[:, 512 * c:512 * (c + 1)], pp[:]),
                 reads=[ppk], writes=[("qfm",)], tag="Qevac")

        if g == 0:
            p.dbg("kfm", kfm[:], [("kfm",)], [128, 6144], BF16)
            p.dbg("qfm", qfm[:], [("qfm",)], [128, 4096], BF16)
            p.dbg("vau1", vau[1][:], [("vau", 1)], [128, 2048], BF16)
        for h in range(H):
            c, base = h // 2, 64 * (h % 2)
            for j, s in enumerate(slots):
                pp = pS[j % 2]
                ppk = ("ps", "pA" if j % 2 == 0 else "pB")
                p.op("tensor",
                     lambda e, pp=pp, c=c, base=base, s=s:
                     e.matmul(pp[:],
                              kfm[base:base + 64,
                                  768 * c + 128 * s:768 * c + 128 * (s + 1)],
                              qfm[base:base + 64, 512 * c:512 * (c + 1)],
                              start=True, stop=True),
                     reads=[("kfm",), ("qfm",)], writes=[ppk], tag="Smm")
                pb = pbf[j % 2]
                pbk = ("pbf", j % 2)
                p.op("scalar",
                     lambda e, pp=pp, pb=pb: e.activation(
                         pb[:], pp[:], AF.Exp, bias=biasc[:, 0:1], scale=0.125),
                     reads=[ppk, ("biasc",)], writes=[pbk], tag="exp")
                p.op("vector",
                     lambda e, pb=pb, s=s: e.tensor_mul(
                         pb[:], pb[:], C["L0M"][:, 640 - 128 * s:1152 - 128 * s]),
                     reads=[pbk, ("c", "L0M")], writes=[pbk], tag="maskmul")
                p.op("tensor",
                     lambda e, pb=pb, s=s, h=h, j=j:
                     e.matmul(pC_[:], vau[s][:, 128 * h:128 * (h + 1)],
                              pb[:], start=(j == 0), stop=(j == len(slots) - 1)),
                     reads=[pbk, ("vau", s)], writes=[("ps", "pC")], tag="PV")
            p.op("scalar",
                 lambda e: e.activation(lnl[0:64, :], pC_[64:128, :], AF.Ln,
                                        bias=epsl[0:64, 0:1]),
                 reads=[("ps", "pC"), ("epsl",)], writes=[("lnl",)], tag="lnl")
            p.op("scalar",
                 lambda e: e.activation(rbf[0:64, :], lnl[0:64, :], AF.Exp,
                                        scale=-1.0),
                 reads=[("lnl",)], writes=[("rbf",)], tag="rexp")
            p.op("scalar",
                 lambda e: e.copy(cxf[0:64, :], pC_[0:64, :]),
                 reads=[("ps", "pC")], writes=[("cxf",)], tag="ctxevac")
            p.op("vector",
                 lambda e, c=c, base=base: e.tensor_mul(
                     ctx[base:base + 64, 512 * c:512 * (c + 1)],
                     cxf[0:64, :], rbf[0:64, :]),
                 reads=[("cxf",), ("rbf",)], writes=[("ctx",)], tag="ctxmul")
            if g == 0 and h == 0:
                p.dbg("pbf0", pbf[len(slots) % 2][:],
                      [("pbf", len(slots) % 2)], [128, 512], BF16)


        if g == 0:
            p.dbg("ctx", ctx[:], [("ctx",)], [128, 4096], BF16)
        for qt in range(4):
            s = qt + 1
            for half in range(2):
                pp = pM[half]
                ppk = ("ps", "pD" if half == 0 else "pE")
                for c in range(8):
                    p.op("tensor",
                         lambda e, pp=pp, c=c, qt=qt, half=half:
                         e.matmul(pp[:],
                                  ctx[:, 512 * c + 128 * qt:512 * c + 128 * (qt + 1)],
                                  wproj[:, 1024 * c + 512 * half:
                                        1024 * c + 512 * (half + 1)],
                                  start=(c == 0), stop=False),
                         reads=[("ctx",), ("bf8kA",)], writes=[ppk], tag="Omm")
                p.op("tensor",
                     lambda e, pp=pp, half=half: e.matmul(
                         pp[:], ones_bf[0:1, :],
                         bproj[0:1, 512 * half:512 * (half + 1)],
                         start=False, stop=True),
                     reads=[("biasrow",), ("c", "ONESBF")], writes=[ppk],
                     tag="Obias")
                p.op("scalar",
                     lambda e, pp=pp, half=half: e.copy(
                         att[:, 512 * half:512 * (half + 1)], pp[:]),
                     reads=[ppk], writes=[("att",)], tag="Oevac")
            if g == 0 and qt == 0:
                p.dbg("att", att[:], [("att",)], [128, 1024], F32)
            xt, xk = xin[s]
            p.op("vector",
                 lambda e, xt=xt: e.tensor_add(yy[:], xt[:], att[:]),
                 reads=[xk, ("att",)], writes=[("yy",)], tag="resid")
            kt = kt0 + s
            emit_ln_epilogue(p, pool, ("yy",), yy,
                             out_dram[128 * kt:128 * (kt + 1), :],
                             ("out", kt), sq_t=att, sq_k=("att",),
                             yln_t=xt, yln_k=xk)


def emit_cast_transpose_tile(p, pool, C, xt_ap, xin_key, xfm, xfm_key,
                             chunk_stride, tok_off, xbf_slot):
    nc = p.nc
    pT = pool.ps("pT", 1024, BF16)
    xb = pool.sb(f"xbf{xbf_slot}", 1024, BF16)
    xbk = ("xbf", xbf_slot)
    p.op("vector", lambda e: e.tensor_copy(xb[:], xt_ap),
         reads=[xin_key], writes=[xbk], tag="xcast")
    for c in range(8):
        p.op("tensor",
             lambda e, c=c: e.transpose(pT[:, 128 * c:128 * (c + 1)],
                                        xb[:, 128 * c:128 * (c + 1)],
                                        C["IDENT"][:]),
             reads=[xbk, ("c", "IDENT")], writes=[("ps", "pT")], tag="xT")
    p.op("scalar",
         lambda e: e.copy(
             xfm[:].rearrange("p (c q) -> p c q", q=chunk_stride)
             [:, :, tok_off:tok_off + 128],
             pT[:].rearrange("p (c q) -> p c q", q=128)),
         reads=[("ps", "pT")], writes=[xfm_key], tag="xevac")


def emit_moe(p, pool, C, x_dram, out_dram, wr_d, wg_d, wu_d, wd_d,
             sg_d, su_d, sd_d, ebias_d, tiles=range(4)):
    nc = p.nc
    E = 8
    wr = pool.sb("wr", 64, BF16)
    stg = pool.sb("stage", 2048, F32)
    p.dma("gpsimd", stg[:, 0:64].rearrange("p (c n) -> p c n", n=8),
          wr_d.rearrange("(c p) n -> p c n", p=128), writes=[("stage",)])
    p.op("vector", lambda e: e.tensor_copy(wr[:, 0:64], stg[:, 0:64]),
         reads=[("stage",)], writes=[("wr",)])
    ebias = pool.sb("ebias", 8, BF16, parts=1)
    p.dma("gpsimd", stg[0:1, 64:72], ebias_d.rearrange("(a n) -> a n", a=1),
          writes=[("stage",)], tag="ebload")
    p.op("vector", lambda e: e.tensor_copy(ebias[0:1, :], stg[0:1, 64:72]),
         reads=[("stage",)], writes=[("ebias",)])
    sgb = pool.sb("sgb", 8192, BF16)
    sub = pool.sb("sub", 8192, BF16)
    sdb = pool.sb("sdb", 8192, BF16)
    load_weight_bf16(p, pool, sgb, ("sgb",), sg_d, 1024, 1024, "stage",
                     stage_cols=2048)
    load_weight_bf16(p, pool, sub, ("sub",), su_d, 1024, 1024, "stage",
                     stage_cols=2048)
    load_weight_bf16(p, pool, sdb, ("sdb",), sd_d, 1024, 1024, "stage",
                     stage_cols=2048)

    xfm = pool.sb("xfm8", 8192, BF16)
    hfm = pool.sb("hfm", 4096, BF16)
    hsf = pool.sb("hsf", 8192, BF16)
    acc = pool.sb("acc", 8192, F32)
    wgb = [pool.sb(f"wgb{i}", 4096, BF16) for i in range(2)]
    wub = [pool.sb(f"wub{i}", 4096, BF16) for i in range(2)]
    wdb = [pool.sb(f"wdb{i}", 4096, BF16) for i in range(2)]
    stg2 = [pool.sb("stage", 2048, F32), pool.sb("stage", 2048, F32)]
    names = ["lg", "dd1", "aa1", "mm1", "ii1", "cc1", "dd2", "ab2", "mm2",
             "is1", "lm", "dd3", "aa3", "mm3", "ii3", "cc3", "dd4", "ab4",
             "mm4", "is2", "tt1", "tt2"]
    rt = {nm: pool.sb("rt_" + nm, 8, F32) for nm in names}
    rs = {nm: pool.sb("rs_" + nm, 1, F32) for nm in
          ["r1", "px1", "r2", "px2", "sd_", "th", "pp2", "pp1"]}
    rw = pool.sb("rw_all", 64, F32)
    sg_f = pool.sb("sg_f", 512, F32)
    up_f = pool.sb("up_f", 512, F32)
    tmp_f = pool.sb("tmp_f", 512, F32)
    xt2 = [pool.sb(f"xin{i}", 1024, F32) for i in range(2)]
    xep = xt2

    pM = [pool.ps("pD"), pool.ps("pE")]
    pR = pool.ps("pF")
    ones_bf = C["ONESBF"]

    def load_expert(e_, par):
        stgt = stg2[par]
        stk = ("stage",)
        for nm, dram, dstt, rows, cols in [
            ("wgb", wg_d, wgb[par], 1024, 512),
            ("wub", wu_d, wub[par], 1024, 512),
            ("wdb", wd_d, wdb[par], 512, 1024),
        ]:
            nch = rows // 128
            per = 2048 // cols
            c0 = 0
            while c0 < nch:
                cn = min(per, nch - c0)
                src = dram[e_].rearrange("(c p) n -> p c n", p=128)[:, c0:c0 + cn, :]
                p.dma("gpsimd",
                      stgt[:, 0:cn * cols].rearrange("p (c n) -> p c n", n=cols),
                      src, writes=[stk], tag="weload")
                p.op("vector",
                     lambda e, c0=c0, cn=cn, cols=cols, dstt=dstt:
                     e.tensor_copy(dstt[:, cols * c0:cols * (c0 + cn)],
                                   stgt[:, 0:cols * cn]),
                     reads=[stk], writes=[(nm, par)], tag="wecast")
                c0 += cn

    for ti, tt in enumerate(tiles):
        for sub_ in range(8):
            kt = 8 * tt + sub_
            xt = xt2[sub_ % 2]
            xk = ("xin", sub_ % 2)
            p.dma("gpsimd", xt[:], x_dram[128 * kt:128 * (kt + 1), :],
                  writes=[xk], tag="xload")
            emit_cast_transpose_tile(p, pool, C, xt[:], xk, xfm, ("xfm",),
                                     1024, 128 * sub_, sub_ % 2)
        p.op("gpsimd", lambda e: e.memset(acc[:], 0.0), writes=[("acc",)],
             tag="acczero")
        for sub_ in range(8):
            for c in range(8):
                p.op("tensor",
                     lambda e, c=c, sub_=sub_: e.matmul(
                         pR[:, 0:8],
                         xfm[:, 1024 * c + 128 * sub_:1024 * c + 128 * (sub_ + 1)],
                         wr[:, 8 * c:8 * (c + 1)],
                         start=(c == 0), stop=False),
                     reads=[("xfm",), ("wr",)], writes=[("ps", "pF")], tag="Rmm")
            p.op("tensor",
                 lambda e: e.matmul(pR[:, 0:8], ones_bf[0:1, :], ebias[0:1, :],
                                    start=False, stop=True),
                 reads=[("ebias",), ("c", "ONESBF")], writes=[("ps", "pF")],
                 tag="Rbias")
            p.op("vector", lambda e: e.tensor_copy(rt["lg"][:, 0:8], pR[:, 0:8]),
                 reads=[("ps", "pF")], writes=[("rt", "lg")], tag="Revac")
            t = rt
            def vop(fn, outs, ins, tag):
                p.op("vector", fn, reads=[("rt", n) for n in ins],
                     writes=[("rt", n) for n in outs], tag=tag)
            vop(lambda e: e.reduce_max(rs["r1"][:], t["lg"][:, 0:8], AX.X),
                ["r1"], ["lg"], "t1max")
            vop(lambda e: e.tensor_scalar(t["dd1"][:, 0:8], t["lg"][:, 0:8],
                                          rs["r1"][:, 0:1], None, OP.subtract),
                ["dd1"], ["lg", "r1"], "t1d")
            p.op("scalar", lambda e: e.activation(t["aa1"][:, 0:8],
                                                  t["dd1"][:, 0:8], AF.Abs,
                                                  scale=1.0e38),
                 reads=[("rt", "dd1")], writes=[("rt", "aa1")], tag="t1abs")
            vop(lambda e: e.tensor_scalar(t["mm1"][:, 0:8], t["aa1"][:, 0:8],
                                          1.0, -1.0, OP.min, OP.mult),
                ["mm1"], ["aa1"], "t1min")
            vop(lambda e: e.tensor_scalar(t["ii1"][:, 0:8], t["mm1"][:, 0:8],
                                          1.0, None, OP.add),
                ["ii1"], ["mm1"], "t1ind")
            vop(lambda e: e.scalar_tensor_tensor(t["cc1"][:, 0:8],
                                                 t["ii1"][:, 0:8], -16.0,
                                                 C["IOTA9"][:, 0:8], OP.mult,
                                                 OP.add),
                ["cc1"], ["ii1"], "t1cand")
            vop(lambda e: e.tensor_reduce(rs["px1"][:], t["cc1"][:, 0:8], AX.X,
                                          OP.min),
                ["px1"], ["cc1"], "t1idx")
            vop(lambda e: e.tensor_scalar(t["dd2"][:, 0:8], C["IOTA"][:, 0:8],
                                          rs["px1"][:, 0:1], None, OP.subtract),
                ["dd2"], ["px1"], "t1d2")
            p.op("scalar", lambda e: e.activation(t["ab2"][:, 0:8],
                                                  t["dd2"][:, 0:8], AF.Abs),
                 reads=[("rt", "dd2")], writes=[("rt", "ab2")], tag="t1abs2")
            vop(lambda e: e.tensor_scalar(t["mm2"][:, 0:8], t["ab2"][:, 0:8],
                                          1.0, -1.0, OP.min, OP.mult),
                ["mm2"], ["ab2"], "t1min2")
            vop(lambda e: e.tensor_scalar(t["is1"][:, 0:8], t["mm2"][:, 0:8],
                                          1.0, None, OP.add),
                ["is1"], ["mm2"], "t1is")
            vop(lambda e: e.scalar_tensor_tensor(t["lm"][:, 0:8],
                                                 t["is1"][:, 0:8], -1.0e30,
                                                 t["lg"][:, 0:8], OP.mult,
                                                 OP.add),
                ["lm"], ["is1", "lg"], "t2mask")
            vop(lambda e: e.reduce_max(rs["r2"][:], t["lm"][:, 0:8], AX.X),
                ["r2"], ["lm"], "t2max")
            vop(lambda e: e.tensor_scalar(t["dd3"][:, 0:8], t["lm"][:, 0:8],
                                          rs["r2"][:, 0:1], None, OP.subtract),
                ["dd3"], ["lm", "r2"], "t2d")
            p.op("scalar", lambda e: e.activation(t["aa3"][:, 0:8],
                                                  t["dd3"][:, 0:8], AF.Abs,
                                                  scale=1.0e38),
                 reads=[("rt", "dd3")], writes=[("rt", "aa3")], tag="t2abs")
            vop(lambda e: e.tensor_scalar(t["mm3"][:, 0:8], t["aa3"][:, 0:8],
                                          1.0, -1.0, OP.min, OP.mult),
                ["mm3"], ["aa3"], "t2min")
            vop(lambda e: e.tensor_scalar(t["ii3"][:, 0:8], t["mm3"][:, 0:8],
                                          1.0, None, OP.add),
                ["ii3"], ["mm3"], "t2ind")
            vop(lambda e: e.scalar_tensor_tensor(t["cc3"][:, 0:8],
                                                 t["ii3"][:, 0:8], -16.0,
                                                 C["IOTA9"][:, 0:8], OP.mult,
                                                 OP.add),
                ["cc3"], ["ii3"], "t2cand")
            vop(lambda e: e.tensor_reduce(rs["px2"][:], t["cc3"][:, 0:8], AX.X,
                                          OP.min),
                ["px2"], ["cc3"], "t2idx")
            vop(lambda e: e.tensor_scalar(t["dd4"][:, 0:8], C["IOTA"][:, 0:8],
                                          rs["px2"][:, 0:1], None, OP.subtract),
                ["dd4"], ["px2"], "t2d2")
            p.op("scalar", lambda e: e.activation(t["ab4"][:, 0:8],
                                                  t["dd4"][:, 0:8], AF.Abs),
                 reads=[("rt", "dd4")], writes=[("rt", "ab4")], tag="t2abs2")
            vop(lambda e: e.tensor_scalar(t["mm4"][:, 0:8], t["ab4"][:, 0:8],
                                          1.0, -1.0, OP.min, OP.mult),
                ["mm4"], ["ab4"], "t2min2")
            vop(lambda e: e.tensor_scalar(t["is2"][:, 0:8], t["mm4"][:, 0:8],
                                          1.0, None, OP.add),
                ["is2"], ["mm4"], "t2is")
            vop(lambda e: e.tensor_sub(rs["sd_"][:], rs["r2"][:], rs["r1"][:]),
                ["sd_"], ["r2", "r1"], "pdif")
            p.op("scalar", lambda e: e.activation(rs["th"][:], rs["sd_"][:],
                                                  AF.Tanh, scale=0.5),
                 reads=[("rt", "sd_")], writes=[("rt", "th")], tag="ptanh")
            vop(lambda e: e.tensor_scalar(rs["pp2"][:], rs["th"][:], 0.5, 0.5,
                                          OP.mult, OP.add),
                ["pp2"], ["th"], "pp2")
            vop(lambda e: e.tensor_scalar(rs["pp1"][:], rs["pp2"][:], -1.0, 1.0,
                                          OP.mult, OP.add),
                ["pp1"], ["pp2"], "pp1")
            vop(lambda e: e.tensor_scalar(t["tt1"][:, 0:8], t["is1"][:, 0:8],
                                          rs["pp1"][:, 0:1], None, OP.mult),
                ["tt1"], ["is1", "pp1"], "rw1")
            vop(lambda e: e.tensor_scalar(t["tt2"][:, 0:8], t["is2"][:, 0:8],
                                          rs["pp2"][:, 0:1], None, OP.mult),
                ["tt2"], ["is2", "pp2"], "rw2")
            p.op("vector",
                 lambda e, sub_=sub_: e.tensor_add(
                     rw[:, 8 * sub_:8 * (sub_ + 1)], t["tt1"][:, 0:8],
                     t["tt2"][:, 0:8]),
                 reads=[("rt", "tt1"), ("rt", "tt2")], writes=[("rw",)],
                 tag="rwadd")
        if ti == 0:
            p.dbg("xfm2", xfm[:], [("xfm",)], [128, 8192], BF16)
            p.dbg("rw", rw[:], [("rw",)], [128, 64], F32)
            p.dbg("lg0", rt["lg"][:], [("rt", "lg")], [128, 8], F32)
            for nm in ("aa1", "mm1", "ii1", "cc1", "is1", "lm", "is2"):
                p.dbg("rt_" + nm, rt[nm][:, 0:8], [("rt", nm)], [128, 8], F32)
            for nm in ("r1", "px1", "r2", "px2", "pp1", "pp2"):
                p.dbg("rs_" + nm, rs[nm][:, 0:1], [("rt", nm)], [128, 1], F32)
        load_expert(0, 0)
        for e_ in range(E):
            par = e_ % 2
            if e_ + 1 < E:
                load_expert(e_ + 1, 1 - par)
            wgt, wut, wdt = wgb[par], wub[par], wdb[par]
            wk_ = [("wgb", par), ("wub", par), ("wdb", par)]
            for fc in range(4):
                for half in range(2):
                    for which, wt, wkk, pp, ppk in [
                        (0, wgt, wk_[0], pM[0], ("ps", "pD")),
                        (1, wut, wk_[1], pM[1], ("ps", "pE")),
                    ]:
                        for ic in range(8):
                            p.op("tensor",
                                 lambda e, wt=wt, fc=fc, half=half, ic=ic, pp=pp:
                                 e.matmul(pp[:],
                                          wt[:, 512 * ic + 128 * fc:
                                             512 * ic + 128 * fc + 128],
                                          xfm[:, 1024 * ic + 512 * half:
                                              1024 * ic + 512 * (half + 1)],
                                          start=(ic == 0), stop=(ic == 7)),
                                 reads=[wkk, ("xfm",)], writes=[ppk], tag="GUmm")
                    p.op("scalar",
                         lambda e: e.activation(sg_f[:], pM[0][:], AF.Silu),
                         reads=[("ps", "pD")], writes=[("sg_f",)], tag="silu")
                    p.op("vector",
                         lambda e: e.tensor_copy(up_f[:], pM[1][:]),
                         reads=[("ps", "pE")], writes=[("up_f",)], tag="upevac")
                    p.op("vector",
                         lambda e, fc=fc, half=half: e.tensor_mul(
                             hfm[:, 1024 * fc + 512 * half:
                                 1024 * fc + 512 * (half + 1)],
                             sg_f[:], up_f[:]),
                         reads=[("sg_f",), ("up_f",)], writes=[("hfm",)],
                         tag="hmul")
            for sub_ in range(8):
                for half in range(2):
                    pp = pM[half]
                    ppk = ("ps", "pD" if half == 0 else "pE")
                    for fc in range(4):
                        p.op("tensor",
                             lambda e, fc=fc, sub_=sub_, half=half, pp=pp,
                             wdt=wdt:
                             e.matmul(pp[:],
                                      hfm[:, 1024 * fc + 128 * sub_:
                                          1024 * fc + 128 * (sub_ + 1)],
                                      wdt[:, 1024 * fc + 512 * half:
                                          1024 * fc + 512 * (half + 1)],
                                      start=(fc == 0), stop=(fc == 3)),
                             reads=[wk_[2], ("hfm",)], writes=[ppk], tag="Dmm")
                    p.op("scalar",
                         lambda e, pp=pp, sub_=sub_, e_=e_: e.activation(
                             tmp_f[:], pp[:], AF.Copy,
                             scale=rw[:, 8 * sub_ + e_:8 * sub_ + e_ + 1]),
                         reads=[ppk, ("rw",)], writes=[("tmp_f",)], tag="dscale")
                    p.op("vector",
                         lambda e, sub_=sub_, half=half: e.tensor_add(
                             acc[:, 1024 * sub_ + 512 * half:
                                 1024 * sub_ + 512 * (half + 1)],
                             acc[:, 1024 * sub_ + 512 * half:
                                 1024 * sub_ + 512 * (half + 1)],
                             tmp_f[:]),
                         reads=[("acc",), ("tmp_f",)], writes=[("acc",)],
                         tag="dacc")
        if ti == 0:
            p.dbg("hfm7", hfm[:], [("hfm",)], [128, 4096], BF16)
            p.dbg("accE", acc[:], [("acc",)], [128, 8192], F32)
        for fc in range(8):
            for half in range(2):
                for which, wt, wkk, pp, ppk in [
                    (0, sgb, ("sgb",), pM[0], ("ps", "pD")),
                    (1, sub, ("sub",), pM[1], ("ps", "pE")),
                ]:
                    for ic in range(8):
                        p.op("tensor",
                             lambda e, wt=wt, fc=fc, half=half, ic=ic, pp=pp:
                             e.matmul(pp[:],
                                      wt[:, 1024 * ic + 128 * fc:
                                         1024 * ic + 128 * fc + 128],
                                      xfm[:, 1024 * ic + 512 * half:
                                          1024 * ic + 512 * (half + 1)],
                                      start=(ic == 0), stop=(ic == 7)),
                             reads=[wkk, ("xfm",)], writes=[ppk], tag="SGUmm")
                p.op("scalar",
                     lambda e: e.activation(sg_f[:], pM[0][:], AF.Silu),
                     reads=[("ps", "pD")], writes=[("sg_f",)], tag="ssilu")
                p.op("vector",
                     lambda e: e.tensor_copy(up_f[:], pM[1][:]),
                     reads=[("ps", "pE")], writes=[("up_f",)], tag="supevac")
                p.op("vector",
                     lambda e, fc=fc, half=half: e.tensor_mul(
                         hsf[:, 1024 * fc + 512 * half:
                             1024 * fc + 512 * (half + 1)],
                         sg_f[:], up_f[:]),
                     reads=[("sg_f",), ("up_f",)], writes=[("hsf",)], tag="shmul")
        for sub_ in range(8):
            for half in range(2):
                pp = pM[half]
                ppk = ("ps", "pD" if half == 0 else "pE")
                for fc in range(8):
                    p.op("tensor",
                         lambda e, fc=fc, sub_=sub_, half=half, pp=pp:
                         e.matmul(pp[:],
                                  hsf[:, 1024 * fc + 128 * sub_:
                                      1024 * fc + 128 * (sub_ + 1)],
                                  sdb[:, 1024 * fc + 512 * half:
                                      1024 * fc + 512 * (half + 1)],
                                  start=(fc == 0), stop=(fc == 7)),
                         reads=[("sdb",), ("hsf",)], writes=[ppk], tag="SDmm")
                p.op("scalar",
                     lambda e, pp=pp: e.copy(tmp_f[:], pp[:]),
                     reads=[ppk], writes=[("tmp_f",)], tag="sdevac")
                p.op("vector",
                     lambda e, sub_=sub_, half=half: e.tensor_add(
                         acc[:, 1024 * sub_ + 512 * half:
                             1024 * sub_ + 512 * (half + 1)],
                         acc[:, 1024 * sub_ + 512 * half:
                             1024 * sub_ + 512 * (half + 1)],
                         tmp_f[:]),
                     reads=[("acc",), ("tmp_f",)], writes=[("acc",)],
                     tag="sdacc")
        if ti == 0:
            p.dbg("accS", acc[:], [("acc",)], [128, 8192], F32)
        for sub_ in range(8):
            kt = 8 * tt + sub_
            xe = xep[sub_ % 2]
            xek = ("xin", sub_ % 2)
            p.dma("gpsimd", xe[:], x_dram[128 * kt:128 * (kt + 1), :],
                  writes=[xek], tag="xreload")
            p.op("vector",
                 lambda e, xe=xe, sub_=sub_: e.tensor_add(
                     acc[:, 1024 * sub_:1024 * (sub_ + 1)], xe[:],
                     acc[:, 1024 * sub_:1024 * (sub_ + 1)]),
                 reads=[xek, ("acc",)], writes=[("acc",)], tag="resid")
            emit_ln_epilogue(p, pool, ("acc",),
                             acc[:, 1024 * sub_:1024 * (sub_ + 1)],
                             out_dram[128 * kt:128 * (kt + 1), :],
                             ("out", kt), sq_t=xe, sq_k=xek,
                             yln_t=xe, yln_k=xek)


def _build_moe(tiles=range(4), debug=False):
    nc = bass.Bass("TRN2", target_bir_lowering=False)
    st = ExitStack()
    x_d = nc.dram_tensor("x", [T, D], F32, kind="ExternalInput").ap()
    wr_d = nc.dram_tensor("Wr", [1024, 8], F32, kind="ExternalInput").ap()
    wg_d = nc.dram_tensor("Wg", [8, 1024, 512], F32, kind="ExternalInput").ap()
    wu_d = nc.dram_tensor("Wu", [8, 1024, 512], F32, kind="ExternalInput").ap()
    wd_d = nc.dram_tensor("Wd", [8, 512, 1024], F32, kind="ExternalInput").ap()
    sg_d = nc.dram_tensor("Sg", [1024, 1024], F32, kind="ExternalInput").ap()
    su_d = nc.dram_tensor("Su", [1024, 1024], F32, kind="ExternalInput").ap()
    sd_d = nc.dram_tensor("Sd", [1024, 1024], F32, kind="ExternalInput").ap()
    eb_d = nc.dram_tensor("ebias", [8], F32, kind="ExternalInput").ap()
    out_d = nc.dram_tensor("out", [T, D], F32, kind="ExternalOutput").ap()
    p = Prog(nc)
    p.debug = debug
    pool = Pool(nc, st)
    for r in ("pA", "pB", "pC", "pD", "pE", "pF", "pT"):
        p.psum_keys.add(("ps", r))
    C = _declare_consts(nc, pool, p,
                        needed={"IDENT", "ONESBF", "IOTA9", "IOTA"})
    p.warm_src = x_d[0:1, 0:32]
    emit_moe(p, pool, C, x_d, out_d, wr_d, wg_d, wu_d, wd_d, sg_d, su_d, sd_d,
             eb_d, tiles=tiles)
    p.finalize_and_emit(st)
    st.close()
    return nc


def run_moe(x_full, wr, wg, wu, wd, sg, su, sd, eb, tiles=range(4),
            debug=False, ncore=NCORE):
    nc = _build_moe(tiles=tiles, debug=debug)
    hc = _host_consts()
    hc = {k: hc[k] for k in ("IDENT", "ONESBF", "IOTA9", "IOTA")}
    in_maps = []
    for b in range(ncore):
        m = {"x": np.ascontiguousarray(x_full[b]), "Wr": wr, "Wg": wg, "Wu": wu,
             "Wd": wd, "Sg": sg, "Su": su, "Sd": sd, "ebias": eb}
        m.update(hc)
        in_maps.append(m)
    res = run_bass_kernel_spmd(nc, in_maps, core_ids=list(range(ncore)))
    if debug:
        return res
    return np.stack([r["out"] for r in res.results])


def emit_l1_attn(p, pool, C, x_dram, out_dram, wq_d, wk_d, wv_d, wo_d, bo_d,
                 qgroups=range(8)):
    nc = p.nc
    H = 16
    wqb = pool.sb("wqb", 8192, BF16)
    load_weight_bf16(p, pool, wqb, ("wqb",), wq_d, 1024, 1024, "stage",
                     stage_cols=2048)
    wkb = pool.sb("wkb", 2048, BF16)
    load_weight_bf16(p, pool, wkb, ("wkb",), wk_d, 1024, 256, "stage",
                     stage_cols=2048)
    wvb = pool.sb("wvb", 2048, BF16)
    load_weight_bf16(p, pool, wvb, ("wvb",), wv_d, 1024, 256, "stage",
                     stage_cols=2048)
    wob = pool.sb("wob", 8192, BF16)
    load_weight_bf16(p, pool, wob, ("wob",), wo_d, 1024, 1024, "stage",
                     stage_cols=2048)
    bob = pool.sb("biasrow", 1024, BF16, parts=1)
    bstg = pool.sb("stage", 2048, F32)
    p.dma("gpsimd", bstg[0:1, 0:1024], bo_d.rearrange("(a n) -> a n", a=1),
          writes=[("stage",)])
    p.op("vector", lambda e: e.tensor_copy(bob[0:1, :], bstg[0:1, 0:1024]),
         reads=[("stage",)], writes=[("biasrow",)])

    biasc = pool.sb("biasc", 1, F32)
    epsl = pool.sb("epsl", 1, F32)
    if not getattr(pool, "_attc_done", False):
        pool._attc_done = True
        p.op("vector", lambda e: e.memset(biasc[:], -25.0), writes=[("biasc",)])
        p.op("vector", lambda e: e.memset(epsl[:], 1.0e-26), writes=[("epsl",)])

    xfm = pool.sb("xfmS", 4096, BF16)
    kfmL = pool.sb("kfmL", 16384, BF16)
    vauL = pool.sb("vauL", 16384, BF16)
    qfm = pool.sb("qfm", 4096, BF16)
    ctx = pool.sb("ctxfm", 4096, BF16)
    pbf = [pool.sb(f"pbf{i}", 512, BF16) for i in range(2)]
    lnl = pool.sb("lnl", 512, F32, parts=64)
    rbf = pool.sb("rbf", 512, F32, parts=64)
    cxf = pool.sb("cxf", 512, F32, parts=64)
    att = pool.sb("att", 1024, F32)
    yy = pool.sb("yy", 1024, F32)
    xq = [pool.sb(f"xin{i}", 1024, F32) for i in range(4)]

    pS = [pool.ps("pA"), pool.ps("pB")]
    pC_ = pool.ps("pC")
    pM = [pool.ps("pD"), pool.ps("pE")]
    ones_bf = C["ONESBF"]

    for kg in range(8):
        for sub_ in range(4):
            kt = 4 * kg + sub_
            xt = xq[sub_]
            xk = ("xin", sub_)
            p.dma("gpsimd", xt[:], x_dram[128 * kt:128 * (kt + 1), :],
                  writes=[xk], tag="kxload")
            emit_cast_transpose_tile(p, pool, C, xt[:], xk, xfm, ("xfmS",),
                                     512, 128 * sub_, sub_ % 2)
        for kc in range(2):
            pp = pM[kc]
            ppk = ("ps", "pD" if kc == 0 else "pE")
            for ic in range(8):
                p.op("tensor",
                     lambda e, kc=kc, ic=ic, pp=pp: e.matmul(
                         pp[:], wkb[:, 256 * ic + 128 * kc:256 * ic + 128 * (kc + 1)],
                         xfm[:, 512 * ic:512 * (ic + 1)],
                         start=(ic == 0), stop=(ic == 7)),
                     reads=[("wkb",), ("xfmS",)], writes=[ppk], tag="Kmm")
            for gg in range(2):
                g_ = 2 * kc + gg
                for dsth in range(2):
                    p.op("scalar",
                         lambda e, pp=pp, gg=gg, g_=g_, dsth=dsth, kg=kg:
                         e.copy(kfmL[64 * dsth:64 * (dsth + 1),
                                     4096 * g_ + 512 * kg:4096 * g_ + 512 * (kg + 1)],
                                pp[64 * gg:64 * (gg + 1), :]),
                         reads=[ppk], writes=[("kfmL",)], tag="Kevac")
        for sub_ in range(4):
            kt = 4 * kg + sub_
            pp = pM[0]
            ppk = ("ps", "pD")
            for ic in range(8):
                p.op("tensor",
                     lambda e, sub_=sub_, ic=ic, pp=pp: e.matmul(
                         pp[:, 0:256],
                         xfm[:, 512 * ic + 128 * sub_:512 * ic + 128 * (sub_ + 1)],
                         wvb[:, 256 * ic:256 * (ic + 1)],
                         start=(ic == 0), stop=(ic == 7)),
                     reads=[("wvb",), ("xfmS",)], writes=[ppk], tag="Vmm")
            p.op("vector",
                 lambda e, kt=kt, pp=pp: e.tensor_copy(
                     vauL[:, 512 * kt:512 * (kt + 1)]
                     .rearrange("p (g d) -> p g d", d=128)[:, :, 0:64],
                     pp[:, 0:256].rearrange("p (g d) -> p g d", d=64)),
                 reads=[ppk], writes=[("vauL",)], tag="Vevac")
            p.op("gpsimd",
                 lambda e, kt=kt: e.memset(
                     vauL[:, 512 * kt:512 * (kt + 1)]
                     .rearrange("p (g d) -> p g d", d=128)[:, :, 64:128], 1.0),
                 writes=[("vauL",)], tag="Vones")

    for qg in qgroups:
        for sub_ in range(4):
            qt = 4 * qg + sub_
            xt = xq[sub_]
            xk = ("xin", sub_)
            p.dma("gpsimd", xt[:], x_dram[128 * qt:128 * (qt + 1), :],
                  writes=[xk], tag="qxload")
            emit_cast_transpose_tile(p, pool, C, xt[:], xk, xfm, ("xfmS",),
                                     512, 128 * sub_, sub_ % 2)
        for qc in range(8):
            pp = pM[qc % 2]
            ppk = ("ps", "pD" if qc % 2 == 0 else "pE")
            for ic in range(8):
                p.op("tensor",
                     lambda e, qc=qc, ic=ic, pp=pp: e.matmul(
                         pp[:], wqb[:, 1024 * ic + 128 * qc:1024 * ic + 128 * qc + 128],
                         xfm[:, 512 * ic:512 * (ic + 1)],
                         start=(ic == 0), stop=(ic == 7)),
                     reads=[("wqb",), ("xfmS",)], writes=[ppk], tag="Qmm")
            p.op("scalar",
                 lambda e, qc=qc, pp=pp: e.copy(qfm[:, 512 * qc:512 * (qc + 1)],
                                                pp[:]),
                 reads=[ppk], writes=[("qfm",)], tag="Qevac")
        for h in range(H):
            g = h // 4
            kc, kbase = g // 2, 64 * (g % 2)
            qc, qbase = h // 2, 64 * (h % 2)
            nkt = 4 * qg + 4
            for kt in range(nkt):
                diag = kt >= 4 * qg
                o = 128 * (kt - 4 * qg) if diag else 0
                pp = pS[kt % 2]
                ppk = ("ps", "pA" if kt % 2 == 0 else "pB")
                p.op("tensor",
                     lambda e, pp=pp, g=g, qc=qc, qbase=qbase,
                     kt=kt, o=o: e.matmul(
                         pp[:, o:512],
                         kfmL[qbase:qbase + 64,
                              4096 * g + 128 * kt:4096 * g + 128 * (kt + 1)],
                         qfm[qbase:qbase + 64, 512 * qc + o:512 * (qc + 1)],
                         start=True, stop=True),
                     reads=[("kfmL",), ("qfm",)], writes=[ppk], tag="Smm")
                pb = pbf[kt % 2]
                pbk = ("pbf", kt % 2)
                p.op("scalar",
                     lambda e, pp=pp, pb=pb, o=o: e.activation(
                         pb[:, o:512], pp[:, o:512], AF.Exp,
                         bias=biasc[:, 0:1], scale=0.125),
                     reads=[ppk, ("biasc",)], writes=[pbk], tag="exp")
                if diag:
                    p.op("vector",
                         lambda e, pb=pb, o=o: e.tensor_mul(
                             pb[:, o:o + 128], pb[:, o:o + 128],
                             C["L1M"][:, 512:640]),
                         reads=[pbk, ("c", "L1M")], writes=[pbk], tag="maskmul")
                p.op("tensor",
                     lambda e, pb=pb, kt=kt, g=g, o=o, nkt=nkt: e.matmul(
                         pC_[:, o:512],
                         vauL[:, 512 * kt + 128 * g:512 * kt + 128 * (g + 1)],
                         pb[:, o:512], start=(kt == 0), stop=(kt == nkt - 1)),
                     reads=[pbk, ("vauL",)], writes=[("ps", "pC")], tag="PV")
            p.op("scalar",
                 lambda e: e.activation(lnl[0:64, :], pC_[64:128, :], AF.Ln,
                                        bias=epsl[0:64, 0:1]),
                 reads=[("ps", "pC"), ("epsl",)], writes=[("lnl",)], tag="lnl")
            p.op("scalar",
                 lambda e: e.activation(rbf[0:64, :], lnl[0:64, :], AF.Exp,
                                        scale=-1.0),
                 reads=[("lnl",)], writes=[("rbf",)], tag="rexp")
            p.op("scalar",
                 lambda e: e.copy(cxf[0:64, :], pC_[0:64, :]),
                 reads=[("ps", "pC")], writes=[("cxf",)], tag="ctxevac")
            p.op("vector",
                 lambda e, qc=qc, qbase=qbase: e.tensor_mul(
                     ctx[qbase:qbase + 64, 512 * qc:512 * (qc + 1)],
                     cxf[0:64, :], rbf[0:64, :]),
                 reads=[("cxf",), ("rbf",)], writes=[("ctx",)], tag="ctxmul")
        for qt in range(4):
            for half in range(2):
                pp = pM[half]
                ppk = ("ps", "pD" if half == 0 else "pE")
                for c in range(8):
                    p.op("tensor",
                         lambda e, pp=pp, c=c, qt=qt, half=half:
                         e.matmul(pp[:],
                                  ctx[:, 512 * c + 128 * qt:512 * c + 128 * (qt + 1)],
                                  wob[:, 1024 * c + 512 * half:
                                      1024 * c + 512 * (half + 1)],
                                  start=(c == 0), stop=False),
                         reads=[("ctx",), ("wob",)], writes=[ppk], tag="Omm")
                p.op("tensor",
                     lambda e, pp=pp, half=half: e.matmul(
                         pp[:], ones_bf[0:1, :],
                         bob[0:1, 512 * half:512 * (half + 1)],
                         start=False, stop=True),
                     reads=[("biasrow",), ("c", "ONESBF")], writes=[ppk],
                     tag="Obias")
                p.op("scalar",
                     lambda e, pp=pp, half=half: e.copy(
                         att[:, 512 * half:512 * (half + 1)], pp[:]),
                     reads=[ppk], writes=[("att",)], tag="Oevac")
            xt = xq[qt]
            xk = ("xin", qt)
            p.op("vector",
                 lambda e, xt=xt: e.tensor_add(yy[:], xt[:], att[:]),
                 reads=[xk, ("att",)], writes=[("yy",)], tag="resid")
            ktile = 4 * qg + qt
            emit_ln_epilogue(p, pool, ("yy",), yy,
                             out_dram[128 * ktile:128 * (ktile + 1), :],
                             ("out", ktile), sq_t=att, sq_k=("att",),
                             yln_t=xt, yln_k=xk)


def emit_l1_ffn(p, pool, C, x_dram, out_dram, wT_d, b_d, ld_d, lu_d, daux_d,
                uauxT_d):
    nc = p.nc
    lu = pool.sb("lu", 4, BF16, parts=50)
    ld = pool.sb("ld", 128, BF16, parts=4)
    da = pool.sb("da", 1024, BF16, parts=100)
    ua = pool.sb("ua", 1024, BF16, parts=50)
    stg = pool.sb("stage", 2048, F32)
    for nm, dram, dst, pr, cc in [("lu", lu_d, lu, 50, 4), ("ld", ld_d, ld, 4, 100),
                                  ("da", daux_d, da, 100, 1024),
                                  ("ua", uauxT_d, ua, 50, 1024)]:
        p.dma("gpsimd", stg[0:pr, 0:cc], dram, writes=[("stage",)], tag="lload")
        p.op("vector",
             lambda e, dst=dst, pr=pr, cc=cc: e.tensor_copy(dst[0:pr, 0:cc],
                                                            stg[0:pr, 0:cc]),
             reads=[("stage",)], writes=[(nm,)], tag="lcast")
    k1 = pool.sb("k1", 1024, BF16, parts=4)
    k2 = pool.sb("k2", 1024, BF16, parts=100)
    wc = pool.sb("wcb", 8192, BF16)
    att4 = pool.sb("att", 1024, F32)
    yy4 = pool.sb("yy", 1024, F32)
    bb = pool.sb("biasrow", 1024, BF16, parts=1)
    p.dma("gpsimd", stg[0:1, 0:1024], b_d.rearrange("(a n) -> a n", a=1),
          writes=[("stage",)])
    p.op("vector", lambda e: e.tensor_copy(bb[0:1, :], stg[0:1, 0:1024]),
         reads=[("stage",)], writes=[("biasrow",)])
    pM = [pool.ps("pD"), pool.ps("pE")]
    for half in range(2):
        p.op("tensor",
             lambda e, half=half: e.matmul(
                 pM[half][0:4, :], lu[0:50, 0:4], ua[0:50, 512 * half:512 * (half + 1)],
                 start=True, stop=True),
             reads=[("lu",), ("ua",)], writes=[("ps", "pD" if half == 0 else "pE")],
             tag="k1mm")
        p.op("vector",
             lambda e, half=half: e.tensor_copy(k1[0:4, 512 * half:512 * (half + 1)],
                                                pM[half][0:4, :]),
             reads=[("ps", "pD" if half == 0 else "pE")], writes=[("k1",)],
             tag="k1evac")
    for half in range(2):
        p.op("tensor",
             lambda e, half=half: e.matmul(
                 pM[half][0:100, :], ld[0:4, 0:100],
                 k1[0:4, 512 * half:512 * (half + 1)], start=True, stop=True),
             reads=[("ld",), ("k1",)], writes=[("ps", "pD" if half == 0 else "pE")],
             tag="k2mm")
        p.op("vector",
             lambda e, half=half: e.tensor_copy(k2[0:100, 512 * half:512 * (half + 1)],
                                                pM[half][0:100, :]),
             reads=[("ps", "pD" if half == 0 else "pE")], writes=[("k2",)],
             tag="k2evac")
    for c in range(8):
        for half in range(2):
            pp = pM[half]
            ppk = ("ps", "pD" if half == 0 else "pE")
            p.op("tensor",
                 lambda e, c=c, half=half, pp=pp: e.matmul(
                     pp[:], da[0:100, 128 * c:128 * (c + 1)],
                     k2[0:100, 512 * half:512 * (half + 1)],
                     start=True, stop=True),
                 reads=[("da",), ("k2",)], writes=[ppk], tag="Rmm2")
        p.dma("gpsimd",
              stg[:, 0:1024],
              wT_d[128 * c:128 * (c + 1), :], writes=[("stage",)], tag="wTload")
        for half in range(2):
            pp = pM[half]
            ppk = ("ps", "pD" if half == 0 else "pE")
            p.op("scalar",
                 lambda e, pp=pp, half=half: e.copy(
                     att4[:, 512 * half:512 * (half + 1)], pp[:]),
                 reads=[ppk], writes=[("att4",)], tag="Revac")
        p.op("vector",
             lambda e, c=c: e.tensor_add(wc[:, 1024 * c:1024 * (c + 1)],
                                         stg[:, 0:1024], att4[:, 0:1024]),
             reads=[("stage",), ("att4",)], writes=[("wcb",)], tag="Wcomb")
    xfm = pool.sb("xfmS", 4096, BF16)
    for tg in range(8):
        for sub_ in range(4):
            kt = 4 * tg + sub_
            xt = pool.sb(f"xin{sub_}", 1024, F32)
            xk = ("xin", sub_)
            p.dma("gpsimd", xt[:], x_dram[128 * kt:128 * (kt + 1), :],
                  writes=[xk], tag="fxload")
            emit_cast_transpose_tile(p, pool, C, xt[:], xk, xfm, ("xfmS",),
                                     512, 128 * sub_, sub_ % 2)
        for sub_ in range(4):
            kt = 4 * tg + sub_
            for half in range(2):
                pp = pM[half]
                ppk = ("ps", "pD" if half == 0 else "pE")
                for ic in range(8):
                    p.op("tensor",
                         lambda e, ic=ic, sub_=sub_, half=half, pp=pp:
                         e.matmul(pp[:],
                                  xfm[:, 512 * ic + 128 * sub_:512 * ic + 128 * (sub_ + 1)],
                                  wc[:, 1024 * ic + 512 * half:1024 * ic + 512 * (half + 1)],
                                  start=(ic == 0), stop=False),
                         reads=[("xfmS",), ("wcb",)], writes=[ppk], tag="Fmm")
                p.op("tensor",
                     lambda e, pp=pp, half=half: e.matmul(
                         pp[:], C["ONESBF"][0:1, :],
                         bb[0:1, 512 * half:512 * (half + 1)],
                         start=False, stop=True),
                     reads=[("biasrow",), ("c", "ONESBF")], writes=[ppk],
                     tag="Fbias")
                p.op("scalar",
                     lambda e, pp=pp, half=half: e.copy(
                         att4[:, 512 * half:512 * (half + 1)], pp[:]),
                     reads=[ppk], writes=[("att4",)], tag="Fevac")
            xt = pool.sb(f"xin{sub_}", 1024, F32)
            xk = ("xin", sub_)
            p.op("vector",
                 lambda e, xt=xt: e.tensor_add(yy4[:], xt[:], att4[:]),
                 reads=[xk, ("att4",)], writes=[("yy4",)], tag="resid")
            emit_ln_epilogue(p, pool, ("yy4",), yy4,
                             out_dram[128 * kt:128 * (kt + 1), :],
                             ("out", kt), sq_t=att4, sq_k=("att4",),
                             yln_t=xt, yln_k=xk)


def _build_l1_attn(qgroups=range(8), debug=False):
    nc = bass.Bass("TRN2", target_bir_lowering=False)
    st = ExitStack()
    x_d = nc.dram_tensor("x", [T, D], F32, kind="ExternalInput").ap()
    wq_d = nc.dram_tensor("Wq", [1024, 1024], F32, kind="ExternalInput").ap()
    wk_d = nc.dram_tensor("Wk", [1024, 256], F32, kind="ExternalInput").ap()
    wv_d = nc.dram_tensor("Wv", [1024, 256], F32, kind="ExternalInput").ap()
    wo_d = nc.dram_tensor("Wo", [1024, 1024], F32, kind="ExternalInput").ap()
    bo_d = nc.dram_tensor("bo", [1024], F32, kind="ExternalInput").ap()
    out_d = nc.dram_tensor("out", [T, D], F32, kind="ExternalOutput").ap()
    p = Prog(nc)
    p.debug = debug
    pool = Pool(nc, st)
    for r in ("pA", "pB", "pC", "pD", "pE", "pF", "pT"):
        p.psum_keys.add(("ps", r))
    C = _declare_consts(nc, pool, p, needed={"IDENT", "ONESBF", "L1M"})
    p.warm_src = x_d[0:1, 0:32]
    emit_l1_attn(p, pool, C, x_d, out_d, wq_d, wk_d, wv_d, wo_d, bo_d,
                 qgroups=qgroups)
    p.finalize_and_emit(st)
    st.close()
    return nc


def run_l1_attn(x_full, wq, wk, wv, wo, bo, qgroups=range(8), debug=False,
                ncore=NCORE):
    nc = _build_l1_attn(qgroups=qgroups, debug=debug)
    hc = _host_consts()
    hc = {k: hc[k] for k in ("IDENT", "ONESBF", "L1M")}
    in_maps = []
    for b in range(ncore):
        m = {"x": np.ascontiguousarray(x_full[b]), "Wq": wq, "Wk": wk,
             "Wv": wv, "Wo": wo, "bo": bo}
        m.update(hc)
        in_maps.append(m)
    res = run_bass_kernel_spmd(nc, in_maps, core_ids=list(range(ncore)))
    if debug:
        return res
    return np.stack([r["out"] for r in res.results])


def _build_l1_ffn(debug=False):
    nc = bass.Bass("TRN2", target_bir_lowering=False)
    st = ExitStack()
    x_d = nc.dram_tensor("x", [T, D], F32, kind="ExternalInput").ap()
    wT_d = nc.dram_tensor("WT", [1024, 1024], F32, kind="ExternalInput").ap()
    b_d = nc.dram_tensor("b", [1024], F32, kind="ExternalInput").ap()
    ld_d = nc.dram_tensor("lora_down", [4, 100], F32, kind="ExternalInput").ap()
    lu_d = nc.dram_tensor("lora_up", [50, 4], F32, kind="ExternalInput").ap()
    da_d = nc.dram_tensor("down_aux", [100, 1024], F32, kind="ExternalInput").ap()
    uaT_d = nc.dram_tensor("up_auxT", [50, 1024], F32, kind="ExternalInput").ap()
    out_d = nc.dram_tensor("out", [T, D], F32, kind="ExternalOutput").ap()
    p = Prog(nc)
    p.debug = debug
    pool = Pool(nc, st)
    for r in ("pA", "pB", "pC", "pD", "pE", "pF", "pT"):
        p.psum_keys.add(("ps", r))
    C = _declare_consts(nc, pool, p, needed={"IDENT", "ONESBF"})
    p.warm_src = x_d[0:1, 0:32]
    emit_l1_ffn(p, pool, C, x_d, out_d, wT_d, b_d, ld_d, lu_d, da_d, uaT_d)
    p.finalize_and_emit(st)
    st.close()
    return nc


def run_l1_ffn(x_full, W, b, lora_down, lora_up, down_aux, up_aux,
               debug=False, ncore=NCORE):
    nc = _build_l1_ffn(debug=debug)
    hc = _host_consts()
    hc = {k: hc[k] for k in ("IDENT", "ONESBF")}
    wT = np.ascontiguousarray(W.T)
    uaT = np.ascontiguousarray(up_aux.T)
    in_maps = []
    for b_ in range(ncore):
        m = {"x": np.ascontiguousarray(x_full[b_]), "WT": wT, "b": b,
             "lora_down": lora_down, "lora_up": lora_up, "down_aux": down_aux,
             "up_auxT": uaT}
        m.update(hc)
        in_maps.append(m)
    res = run_bass_kernel_spmd(nc, in_maps, core_ids=list(range(ncore)))
    if debug:
        return res
    return np.stack([r["out"] for r in res.results])


def _build_l0_attn(groups=range(8), debug=False):
    nc = bass.Bass("TRN2", target_bir_lowering=False)
    st = ExitStack()
    x_d = nc.dram_tensor("x", [T, D], F32, kind="ExternalInput").ap()
    wqkv_d = nc.dram_tensor("w_qkv", [1024, 3072], F32, kind="ExternalInput").ap()
    wproj_d = nc.dram_tensor("w_proj", [1024, 1024], F32, kind="ExternalInput").ap()
    bproj_d = nc.dram_tensor("b_proj", [1024], F32, kind="ExternalInput").ap()
    out_d = nc.dram_tensor("out", [T, D], F32, kind="ExternalOutput").ap()
    p = Prog(nc)
    p.debug = debug
    pool = Pool(nc, st)
    for r in ("pA", "pB", "pC", "pD", "pE", "pF"):
        k = ("ps", r)
        p.psum_keys.add(k)
    p.psum_keys.add(("ps", "pT"))
    C = _declare_consts(nc, pool, p)
    p.warm_src = x_d[0:1, 0:32]
    emit_l0_attn(p, pool, C, x_d, out_d, wqkv_d, wproj_d, bproj_d, groups=groups)
    p.finalize_and_emit(st)
    st.close()
    return nc


def run_l0_attn(x_full, w_qkv, w_proj, b_proj, groups=range(8), debug=False,
                ncore=NCORE):
    nc = _build_l0_attn(groups=groups, debug=debug)
    hc = _host_consts()
    in_maps = []
    for b in range(ncore):
        m = {"x": np.ascontiguousarray(x_full[b]),
             "w_qkv": w_qkv, "w_proj": w_proj, "b_proj": b_proj}
        m.update(hc)
        in_maps.append(m)
    res = run_bass_kernel_spmd(nc, in_maps, core_ids=list(range(ncore)))
    if debug:
        return res
    return np.stack([r["out"] for r in res.results])


def _build_fused(debug=False):
    nc = bass.Bass("TRN2", target_bir_lowering=False)
    gst = ExitStack()
    di = {}
    for name, shape in [
        ("x", [T, D]), ("w_qkv", [1024, 3072]), ("w_proj", [1024, 1024]),
        ("b_proj", [1024]), ("Wr", [1024, 8]), ("Wg", [8, 1024, 512]),
        ("Wu", [8, 1024, 512]), ("Wd", [8, 512, 1024]), ("Sg", [1024, 1024]),
        ("Su", [1024, 1024]), ("Sd", [1024, 1024]), ("ebias", [8]),
        ("Wq", [1024, 1024]), ("Wk", [1024, 256]), ("Wv", [1024, 256]),
        ("Wo", [1024, 1024]), ("bo", [1024]), ("WT", [1024, 1024]),
        ("b", [1024]), ("lora_down", [4, 100]), ("lora_up", [50, 4]),
        ("down_aux", [100, 1024]), ("up_auxT", [50, 1024]),
    ]:
        di[name] = nc.dram_tensor(name, shape, F32, kind="ExternalInput").ap()
    out_d = nc.dram_tensor("out", [T, D], F32, kind="ExternalOutput").ap()
    x1 = nc.dram_tensor("x1buf", [T, D], F32).ap()
    x2 = nc.dram_tensor("x2buf", [T, D], F32).ap()
    x3 = nc.dram_tensor("x3buf", [T, D], F32).ap()
    p = Prog(nc)
    p.debug = debug
    for r in ("pA", "pB", "pC", "pD", "pE", "pF", "pT"):
        p.psum_keys.add(("ps", r))
    cpool = Pool(nc, gst)
    C = _declare_consts(nc, cpool, p)
    p.warm_src = di["x"][0:1, 0:32]
    st1 = ExitStack()
    pool = Pool(nc, st1, prefix="e1_")
    pool.sbufs.update(cpool.sbufs)
    emit_l0_attn(p, pool, C, di["x"], x1, di["w_qkv"], di["w_proj"],
                 di["b_proj"])
    p.finalize_and_emit(st1, warm=True)
    st1.close()
    st2 = ExitStack()
    pool = Pool(nc, st2, prefix="e2_")
    pool.sbufs.update(cpool.sbufs)
    pool._eps_done = False
    emit_moe(p, pool, C, x1, x2, di["Wr"], di["Wg"], di["Wu"], di["Wd"],
             di["Sg"], di["Su"], di["Sd"], di["ebias"])
    p.finalize_and_emit(st2, warm=False)
    st2.close()
    st3 = ExitStack()
    pool = Pool(nc, st3, prefix="e3_")
    pool.sbufs.update(cpool.sbufs)
    pool._eps_done = False
    pool._attc_done = False
    emit_l1_attn(p, pool, C, x2, x3, di["Wq"], di["Wk"], di["Wv"], di["Wo"],
                 di["bo"])
    p.finalize_and_emit(st3, warm=False)
    st3.close()
    st4 = ExitStack()
    pool = Pool(nc, st4, prefix="e4_")
    pool.sbufs.update(cpool.sbufs)
    pool._eps_done = False
    emit_l1_ffn(p, pool, C, x3, out_d, di["WT"], di["b"], di["lora_down"],
                di["lora_up"], di["down_aux"], di["up_auxT"])
    p.finalize_and_emit(st4, warm=False)
    st4.close()
    gst.close()
    return nc


def kernel_fused(**inputs):
    nc = _build_fused()
    hc = _host_consts()
    wT = np.ascontiguousarray(np.asarray(inputs["L1_ffn_W"]).T)
    uaT = np.ascontiguousarray(np.asarray(inputs["L1_ffn_up_aux"]).T)
    in_maps = []
    for bb in range(NCORE):
        m = {"x": np.ascontiguousarray(inputs["L0_attn_x"][bb]),
             "w_qkv": inputs["L0_attn_w_qkv"], "w_proj": inputs["L0_attn_w_proj"],
             "b_proj": inputs["L0_attn_b_proj"], "Wr": inputs["L0_ffn_Wr"],
             "Wg": inputs["L0_ffn_Wg"], "Wu": inputs["L0_ffn_Wu"],
             "Wd": inputs["L0_ffn_Wd"], "Sg": inputs["L0_ffn_Sg"],
             "Su": inputs["L0_ffn_Su"], "Sd": inputs["L0_ffn_Sd"],
             "ebias": inputs["L0_ffn_expert_bias"], "Wq": inputs["L1_attn_Wq"],
             "Wk": inputs["L1_attn_Wk"], "Wv": inputs["L1_attn_Wv"],
             "Wo": inputs["L1_attn_Wo"], "bo": inputs["L1_attn_bo"],
             "WT": wT, "b": inputs["L1_ffn_b"],
             "lora_down": inputs["L1_ffn_lora_down"],
             "lora_up": inputs["L1_ffn_lora_up"],
             "down_aux": inputs["L1_ffn_down_aux"], "up_auxT": uaT}
        m.update(hc)
        in_maps.append(m)
    res = run_bass_kernel_spmd(nc, in_maps, core_ids=list(range(NCORE)))
    return np.stack([r["out"] for r in res.results])


def kernel_unfused(**inputs):
    x1 = run_l0_attn(inputs["L0_attn_x"], inputs["L0_attn_w_qkv"],
                     inputs["L0_attn_w_proj"], inputs["L0_attn_b_proj"])
    x2 = run_moe(x1, inputs["L0_ffn_Wr"], inputs["L0_ffn_Wg"],
                 inputs["L0_ffn_Wu"], inputs["L0_ffn_Wd"], inputs["L0_ffn_Sg"],
                 inputs["L0_ffn_Su"], inputs["L0_ffn_Sd"],
                 inputs["L0_ffn_expert_bias"])
    x3 = run_l1_attn(x2, inputs["L1_attn_Wq"], inputs["L1_attn_Wk"],
                     inputs["L1_attn_Wv"], inputs["L1_attn_Wo"],
                     inputs["L1_attn_bo"])
    out = run_l1_ffn(x3, inputs["L1_ffn_W"], inputs["L1_ffn_b"],
                     inputs["L1_ffn_lora_down"], inputs["L1_ffn_lora_up"],
                     inputs["L1_ffn_down_aux"], inputs["L1_ffn_up_aux"])
    return out


def kernel(**inputs):
    return kernel_fused(**inputs)
